# version 64
# baseline (speedup 1.0000x reference)
import numpy as np
from contextlib import ExitStack

import concourse.bass as bass
import concourse.mybir as mybir
from concourse.alu_op_type import AluOpType
from concourse.bass_utils import run_bass_kernel_spmd

F32 = mybir.dt.float32
BF16 = mybir.dt.bfloat16
F16 = mybir.dt.float16
AF = mybir.ActivationFunctionType
AX = mybir.AxisListType.X
MULT = AluOpType.mult
ADD = AluOpType.add
SUB = AluOpType.subtract
MAX = AluOpType.max
MAXOP = AluOpType.max
MINOP = AluOpType.min

CM = float(3 << 22)
EPS = 1e-5
N_CORES = 8
SEQ_PER_CORE = 2


class Cfg:
    def __init__(self, L=2048, D=1024, H=4096):
        self.L, self.D, self.H = L, D, H
        self.T = SEQ_PER_CORE * L
        self.KD = D // 128
        self.KH = H // 128
        self.TT = self.T // 128
        self.GB = self.T // 512
        self.GD = self.T // 256


class Sem:
    def __init__(self, st, nc, name):
        self.h = st.enter_context(nc.semaphore(name))
        self.n = 0

    def inc(self, instr, k=1):
        instr.then_inc(self.h, k)
        self.n += k
        return self.n


def build(cfg: Cfg, debug=False):
    nc = bass.Bass("TRN2", target_bir_lowering=False,
                   dynamic_dma_scratch_size=2048,
                   detect_race_conditions=False)
    L, D, H, T = cfg.L, cfg.D, cfg.H, cfg.T
    KD, KH, TT, GB, GD = cfg.KD, cfg.KH, cfg.TT, cfg.GB, cfg.GD

    x_in = nc.dram_tensor("x", [T, D], F32, kind="ExternalInput").ap()
    bw_in = nc.dram_tensor("bw", [128, D], F32, kind="ExternalInput").ap()
    cw_in = nc.dram_tensor("cw", [D, 128], F32, kind="ExternalInput").ap()
    la_in = nc.dram_tensor("la", [1, 128], F32, kind="ExternalInput").ap()
    dp_in = nc.dram_tensor("dp", [1, D], F32, kind="ExternalInput").ap()
    wi_in = nc.dram_tensor("wi", [D, H], F32, kind="ExternalInput").ap()
    bi_in = nc.dram_tensor("bi", [KH, 128], F32, kind="ExternalInput").ap()
    wo_in = nc.dram_tensor("wo", [H, D], F32, kind="ExternalInput").ap()
    bo_in = nc.dram_tensor("bo", [1, D], F32, kind="ExternalInput").ap()
    id_in = nc.dram_tensor("ident", [128, 128], F32, kind="ExternalInput").ap()
    y_out = nc.dram_tensor("y", [T, D], F32, kind="ExternalOutput").ap()

    xnt_dram = nc.dram_tensor("xnt_spill", [GB, 128, 4096], F32).ap()
    hid_dram = nc.dram_tensor("hid_spill", [KH, GB, 128, 512], F32,
                              kind="ExternalOutput" if debug else "Internal").ap()
    if debug:
        dbg_small = nc.dram_tensor("dbg_small", [128, 256], F32, kind="ExternalOutput").ap()
        dbg_xproj = nc.dram_tensor("dbg_xproj", [128, T], F32, kind="ExternalOutput").ap()
        dbg_hs = nc.dram_tensor("dbg_hs", [128, T], F32, kind="ExternalOutput").ap()
        dbg_wii = nc.dram_tensor("dbg_wii", [128, KD * H], BF16, kind="ExternalOutput").ap()
        dbg_woi = nc.dram_tensor("dbg_woi", [128, KH * D], BF16, kind="ExternalOutput").ap()
        dbg_movb = nc.dram_tensor("dbg_movb", [128, 4096], F32, kind="ExternalOutput").ap()
    ccx_i = nc.dram_tensor("ccx_i", [1, 8], F32)
    ccx_o = nc.dram_tensor("ccx_o", [1, 8], F32)
    cch_i = nc.dram_tensor("cch_i", [1, 8], F32)
    cch_o = nc.dram_tensor("cch_o", [1, 8], F32)

    with ExitStack() as st:
        E = st.enter_context

        def sb(name, p, f, dt=F32):
            return E(nc.sbuf_tensor(name, [p, f], dt))

        ident = sb("identt", 128, 128)
        ones = sb("ones", 1, 128)
        srow = sb("srow", 1, 256)
        arow = sb("arow", 1, 128)
        acol = sb("acol", 128, 1)
        epsc = sb("epsc", 128, 1)
        cmc = sb("cmc", 128, 1)
        cmn = sb("cmn", 128, 1)
        identb = sb("identb", 128, 128, BF16)
        cwt = sb("cwt", 128, D)
        bwt = sb("bwt", 128, D)
        bint = sb("bint", 128, KH)
        binrow = sb("binrow", KH, 128)
        rowb = sb("rowb", 1, D)
        mu = sb("mu", 128, TT)
        rsig = sb("rsig", 128, TT)
        rss = sb("rss", 128, TT)
        sj = sb("sj", 128, KH)
        si = sb("si", 128, KH)
        dv = sb("dv", 128, KH)
        civ = sb("civ", 128, KH)
        amx = sb("amx", 128, TT)
        amh = sb("amh", 128, KH)
        amhg = sb("amhg", 128, KH)
        scl = sb("scl", 128, 8)
        ccs = sb("ccs", 1, 16)
        bnst = sb("bnst", 128, 12)
        bnag = sb("bnag", 128, 2)
        xproj = sb("xproj", 128, max(T, 2 * D, 4096))
        hs = sb("hs", 128, max(T, D))
        wii = sb("wii", 128, KD * H, BF16)
        woi = sb("woi", 128, KH * D, BF16)
        movb = sb("movb", 128, 4096)
        xbuf = sb("xbuf", 128, 2 * D)
        xtmp = sb("xtmp", 128, D)
        qin = sb("qin", 128, 2 * D)
        hslab = sb("hslab", 128, 4 * 512)

        movh = movb.ap().bitcast(BF16)
        movh2 = xproj[:, 0:4096].bitcast(BF16)
        qin_lo = qin[:, 0:D]
        qt = qin[:, D:2 * D]
        outst = xproj[:, 0:2 * D]
        plF = hslab[:, 0:512]
        plR = hslab[:, 512:1024]
        plB = hslab[:, 1024:2048].bitcast(F16)

        def pF(q):
            return plF[:, 0:512] if q == 0 else plR[:, 0:512]

        def plbp(q, which):
            return plB[:, q * 1024 + which * 512:q * 1024 + (which + 1) * 512]

        psS = [E(nc.psum_tensor(f"psS{i}", [128, 512], F32)) for i in range(4)]
        psB = [E(nc.psum_tensor(f"psB{i}", [128, 1024], F32)) for i in range(2)]

        dX = [Sem(st, nc, f"dX{i}") for i in range(2)]
        dXB = [Sem(st, nc, f"dXB{i}") for i in range(4)]
        dXN = Sem(st, nc, "dXN")
        dXD = Sem(st, nc, "dXD")
        dW = Sem(st, nc, "dW")
        dSP = [Sem(st, nc, f"dSP{i}") for i in range(4)]
        dSR = [Sem(st, nc, f"dSR{i}") for i in range(8)]
        dOUT = Sem(st, nc, "dOUT")
        dCC = Sem(st, nc, "dCC")
        pe = Sem(st, nc, "pe")
        ve = Sem(st, nc, "ve")
        se = Sem(st, nc, "se")
        ge = Sem(st, nc, "ge")

        block = E(nc.Block())
        M = {}
        on_t, on_v, on_s = block.tensor, block.vector, block.scalar
        on_y, on_g = block.sync, block.gpsimd

        def vwait(v):
            v.wait_ge(ve.h, ve.n)

        def swait(s):
            s.wait_ge(se.h, se.n)

        def ld0(sync):
            sync.dma_start(out=ident[:], in_=id_in).then_inc(dW.h, 16)
            sync.dma_start(out=arow[0:1, :], in_=la_in).then_inc(dW.h, 16)
            sync.dma_start(out=binrow[0:KH, :], in_=bi_in).then_inc(dW.h, 16)
            sync.dma_start(out=rowb[0:1, :], in_=dp_in).then_inc(dW.h, 16)
            for k in range(KD):
                sync.dma_start(out=qin[:, k * 128:(k + 1) * 128],
                               in_=cw_in[k * 128:(k + 1) * 128, :]).then_inc(dW.h, 16)
            sync.dma_start(out=xtmp[:, 0:D], in_=bw_in).then_inc(dW.h, 16)
            dW.n += 16 * (5 + KD)
            M["ld0"] = dW.n
        on_y(ld0)

        def c0v(vector):
            vector.wait_ge(dW.h, M["ld0"])
            vwait(vector); ve.inc(vector.memset(ones[:], 1.0))
            vwait(vector); ve.inc(vector.memset(epsc[:], EPS))
            vwait(vector); ve.inc(vector.memset(cmc[:], CM))
            vwait(vector); ve.inc(vector.memset(cmn[:], -CM))
            vwait(vector); ve.inc(vector.tensor_copy(identb[:], ident[:]))
            M["ones"] = ve.n
        on_v(c0v)

        def c0t(tensor):
            tensor.wait_ge(dW.h, M["ld0"])
            tensor.wait_ge(ve.h, M["ones"])
            pe.inc(tensor.transpose(psS[0][:, 0:1], arow[0:1, :], ident[0:1, 0:1]))
            pe.inc(tensor.transpose(psS[1][:, 0:KH], binrow[0:KH, :], ident[0:KH, 0:KH]))
            M["tr_la"] = pe.n
            for h in range(2):
                pe.inc(tensor.matmul(psS[2 + h][:, 0:512], ones[0:1, :],
                                     rowb[0:1, h * 512:(h + 1) * 512],
                                     start=True, stop=True))
            M["dpb"] = pe.n
        on_t(c0t)

        def c0v2(vector):
            vector.wait_ge(pe.h, M["tr_la"])
            vwait(vector); ve.inc(vector.tensor_copy(acol[:, 0:1], psS[0][:, 0:1]))
            vwait(vector); ve.inc(vector.tensor_copy(bint[:, 0:KH], psS[1][:, 0:KH]))
            M["bint"] = ve.n
            vector.wait_ge(pe.h, M["dpb"])
            for h in range(2):
                vwait(vector); ve.inc(vector.tensor_copy(hs[:, h * 512:(h + 1) * 512],
                                          psS[2 + h][:, 0:512]))
            M["dstash"] = ve.n
        on_v(c0v2)

        def c0s(scalar):
            scalar.wait_ge(ve.h, M["bint"])
            swait(scalar); se.inc(scalar.activation(acol[:, 0:1], acol[:, 0:1], AF.Exp))
            M["acol"] = se.n
        on_s(c0s)

        for k in range(KD):
            def cwk(tensor, k=k):
                tensor.wait_ge(ve.h, M["bint"] if k == 0 else M[f"cwt{k-1}"])
                pe.inc(tensor.transpose(psS[0][:, 0:128],
                                        qin[:, k * 128:(k + 1) * 128], ident[:]))
                M[f"cwtr{k}"] = pe.n
            on_t(cwk)

            def cwkv(vector, k=k):
                vector.wait_ge(pe.h, M[f"cwtr{k}"])
                vwait(vector); ve.inc(vector.tensor_copy(cwt[:, k * 128:(k + 1) * 128],
                                          psS[0][:, 0:128]))
                M[f"cwt{k}"] = ve.n
            on_v(cwkv)

        for k in range(KD):
            def bwk(tensor, k=k):
                if k == 0:
                    tensor.wait_ge(ve.h, M["bint"])
                else:
                    tensor.wait_ge(se.h, M[f"bwt{k-1}"])
                pe.inc(tensor.transpose(psS[1][:, 0:128],
                                        xtmp[:, k * 128:(k + 1) * 128], ident[:]))
                M[f"bwtr{k}"] = pe.n
            on_t(bwk)

            def bwkv(scalar, k=k):
                scalar.wait_ge(pe.h, M[f"bwtr{k}"])
                swait(scalar); se.inc(scalar.copy(bwt[:, k * 128:(k + 1) * 128], psS[1][:, 0:128]))
                M[f"bwt{k}"] = se.n
            on_s(bwkv)
        M["bwt"] = se.n
        M["bwtr_last"] = None

        QF = hslab[:, 0:128]
        qslot3 = woi.ap().bitcast(F32)[:, 0:D]

        def qbuf(j):
            s = j % 3
            return qin[:, s * D:(s + 1) * D] if s < 2 else qslot3

        def emit_qld(j):
            def ql(scalar, j=j):
                if j <= 1:
                    scalar.wait_ge(pe.h, M[f"cwtr{KD-1}"])
                elif j >= 3:
                    scalar.wait_ge(ve.h, M[f"qqu{j-3}"])
                buf = qbuf(j)
                scalar.dma_start(
                    out=buf.rearrange("p (k c) -> p k c", k=KD),
                    in_=wi_in[:, j * 128:(j + 1) * 128].rearrange(
                        "(k p) c -> p k c", k=KD),
                ).then_inc(dW.h, 16)
                dW.n += 16
                M[f"qld{j}"] = dW.n
            on_s(ql)

        def emit_qfold(j):
            def qfold(vector, j=j):
                vector.wait_ge(dW.h, M[f"qld{j}"])
                if j >= 1:
                    vector.wait_ge(pe.h, M[f"qtr1_{j-1}"])
                buf = qbuf(j)
                in3 = buf.rearrange("p (k c) -> p c k", k=KD)
                vwait(vector); ve.inc(vector.tensor_reduce(QF[:, 0:128], in3, AX, MAX,
                                            apply_absolute_value=True))
                M[f"qfold{j}"] = ve.n
            on_v(qfold)

        def emit_qtr1(j):
            def qtr1(tensor, j=j):
                tensor.wait_ge(ve.h, M[f"qfold{j}"])
                if j == 0:
                    tensor.wait_ge(se.h, M["bwt"])
                pe.inc(tensor.transpose(psB[0][:, (j % 2) * 128:(j % 2 + 1) * 128],
                                        QF[:, 0:128], ident[:]))
                M[f"qtr1_{j}"] = pe.n
            on_t(qtr1)

        def emit_qred(j):
            def qred(vector, j=j):
                vector.wait_ge(pe.h, M[f"qtr1_{j}"])
                sl = psB[0][:, (j % 2) * 128:(j % 2 + 1) * 128]
                vwait(vector); ve.inc(vector.tensor_reduce(sj[:, j:j + 1], sl, AX, MAX))
                vwait(vector); ve.inc(vector.reciprocal(scl[:, 2:3], sj[:, j:j + 1]))
                vwait(vector); ve.inc(vector.tensor_scalar(sj[:, j:j + 1], scl[:, 2:3],
                                            127.0, None, MULT))
                M[f"qred{j}"] = ve.n
            on_v(qred)

        def emit_qtrs(j):
            def qtrs(tensor, j=j):
                tensor.wait_ge(ve.h, M[f"qred{j}"])
                pe.inc(tensor.transpose(
                    psB[0][0:1, 256 + (j % 2) * 128:384 + (j % 2) * 128],
                    sj[:, j:j + 1], ident[:]))
                M[f"qtrs{j}"] = pe.n
            on_t(qtrs)

        def emit_qcps(j):
            def qcps(vector, j=j):
                vector.wait_ge(pe.h, M[f"qtrs{j}"])
                vwait(vector); ve.inc(vector.tensor_copy(
                    srow[0:1, (j % 2) * 128:(j % 2 + 1) * 128],
                    psB[0][0:1, 256 + (j % 2) * 128:384 + (j % 2) * 128]))
                M[f"qcps{j}"] = ve.n
            on_v(qcps)

        def emit_qbc(j):
            def qbc(tensor, j=j):
                tensor.wait_ge(ve.h, M[f"qcps{j}"])
                pe.inc(tensor.matmul(
                    psB[0][:, 512 + (j % 2) * 128:640 + (j % 2) * 128],
                    ones[0:1, :], srow[0:1, (j % 2) * 128:(j % 2 + 1) * 128],
                    start=True, stop=True))
                M[f"qbc{j}"] = pe.n
            on_t(qbc)

        def emit_qqu(j):
            def qqu(vector, j=j):
                vector.wait_ge(pe.h, M[f"qbc{j}"])
                buf = qbuf(j)
                b3 = buf.rearrange("p (k c) -> p k c", k=KD)
                sbc3 = psB[0][:, 512 + (j % 2) * 128:640 + (j % 2) * 128].rearrange(
                    "p (k c) -> p k c", k=1).broadcast_to([128, KD, 128])
                dst = wii.ap().rearrange("p (k h) -> p k h", k=KD)[
                    :, :, j * 128:(j + 1) * 128]
                vwait(vector); ve.inc(vector.tensor_tensor(b3, b3, sbc3, MULT))
                vwait(vector); ve.inc(vector.tensor_scalar(dst, b3, CM, CM, ADD, SUB))
                M[f"qqu{j}"] = ve.n
            on_v(qqu)


        def emit_tileA_ln(t):
            def axl(sync, t=t):
                if t >= 2:
                    sync.wait_ge(ve.h, M[f"axn{t-2}"])
                s = dX[t % 2]
                sync.dma_start(out=xbuf[:, (t % 2) * D:(t % 2 + 1) * D],
                               in_=x_in[t * 128:(t + 1) * 128, :]).then_inc(s.h, 16)
                s.n += 16
                M[f"axl{t}"] = s.n
            on_y(axl)

            def aln(vector, t=t):
                vector.wait_ge(dX[t % 2].h, M[f"axl{t}"])
                xt = xbuf[:, (t % 2) * D:(t % 2 + 1) * D]
                for cdx in range(D // 512):
                    vwait(vector); ve.inc(vector.bn_stats(bnst[:, 6 * cdx:6 * cdx + 6],
                                           xt[:, cdx * 512:(cdx + 1) * 512]))
                vwait(vector); ve.inc(vector.bn_aggr(bnag[:], bnst[:, 0:6 * (D // 512)]))
                vwait(vector); ve.inc(vector.tensor_copy(mu[:, t:t + 1], bnag[:, 0:1]))
                vwait(vector); ve.inc(vector.tensor_copy(rsig[:, t:t + 1], bnag[:, 1:2]))
                M[f"avar{t}"] = ve.n
            on_v(aln)

            def asd(scalar, t=t):
                scalar.wait_ge(ve.h, M[f"avar{t}"])
                swait(scalar); se.inc(scalar.activation(rsig[:, t:t + 1], rsig[:, t:t + 1],
                                         AF.Sqrt, bias=epsc[:, 0:1]))
                M[f"asd{t}"] = se.n
            on_s(asd)

        def emit_tileA_xn(t):
            def axn(vector, t=t):
                vector.wait_ge(se.h, M[f"asd{t}"])
                if t == 0:
                    vector.wait_ge(pe.h, M[f"bwtr{KD-1}"])
                else:
                    vector.wait_ge(pe.h, M[f"atr{t-1}_1"])
                xt = xbuf[:, (t % 2) * D:(t % 2 + 1) * D]
                vwait(vector); ve.inc(vector.reciprocal(rsig[:, t:t + 1], rsig[:, t:t + 1]))
                vwait(vector); ve.inc(vector.tensor_scalar(xtmp[:, 0:D], xt, mu[:, t:t + 1],
                                            rsig[:, t:t + 1], SUB, MULT))
                vwait(vector); ve.inc(vector.tensor_reduce(amx[:, t:t + 1], xtmp[:, 0:D],
                                            AX, MAX, apply_absolute_value=True))
                M[f"axn{t}"] = ve.n
            on_v(axn)

        def emit_tileA_tr(t):
            g, tt = t // 4, t % 4
            for w in range(2):
                def atr(tensor, t=t, tt=tt, g=g, w=w):
                    tensor.wait_ge(ve.h, M[f"axn{t}"])
                    if t == 0 and w == 0:
                        tensor.wait_ge(ve.h, M[f"cwt{KD-1}"])
                        tensor.wait_ge(se.h, M["bwt"])
                    if tt == 0 and w == 0 and g >= 1:
                        tensor.wait_ge(pe.h, M[f"axp{g-1}"])
                        tensor.wait_ge(dXN.h, M[f"axsp{g-1}"])
                    if t >= 1:
                        tensor.wait_ge(se.h, M[f"atrv{t-1}_{w}"])
                    for kk in range(4):
                        k = w * 4 + kk
                        pe.inc(tensor.transpose(
                            psS[w][:, kk * 128:(kk + 1) * 128],
                            xtmp[:, k * 128:(k + 1) * 128], ident[:]))
                    M[f"atr{t}_{w}"] = pe.n
                on_t(atr)

        def emit_tileA_trv(t):
            tt = t % 4
            for w in range(2):
                def atrv(scalar, t=t, tt=tt, w=w):
                    scalar.wait_ge(pe.h, M[f"atr{t}_{w}"])
                    dst = movb.ap().rearrange("p (k c) -> p k c", k=KD)[
                        :, w * 4:(w + 1) * 4, tt * 128:(tt + 1) * 128]
                    src = psS[w][:, 0:512].rearrange("p (k c) -> p k c", k=4)
                    swait(scalar); se.inc(scalar.copy(dst, src))
                    M[f"atrv{t}_{w}"] = se.n
                on_s(atrv)

        def emit_axp(g):
            def axp(tensor, g=g):
                tensor.wait_ge(se.h, M[f"atrv{g*4+3}_1"])
                if g >= 2:
                    tensor.wait_ge(ve.h, M[f"axpv{g-2}"])
                else:
                    tensor.wait_ge(ve.h, M["dstash"])
                ps = psS[2 + g % 2]
                for k in range(KD):
                    pe.inc(tensor.matmul(ps[:], bwt[:, k * 128:(k + 1) * 128],
                                         movb[:, k * 512:(k + 1) * 512],
                                         start=(k == 0), stop=(k == KD - 1)))
                M[f"axp{g}"] = pe.n
            on_t(axp)

            def axpv(vector, g=g):
                vector.wait_ge(pe.h, M[f"axp{g}"])
                vwait(vector); ve.inc(vector.tensor_copy(xproj[:, g * 512:(g + 1) * 512],
                                          psS[2 + g % 2][:]))
                M[f"axpv{g}"] = ve.n
            on_v(axpv)

            def axsp(sync, g=g):
                sync.wait_ge(se.h, M[f"atrv{g*4+3}_1"])
                sync.dma_start(out=xnt_dram[g], in_=movb[:, 0:4096]).then_inc(dXN.h, 16)
                dXN.n += 16
                M[f"axsp{g}"] = dXN.n
            on_y(axsp)

        for t in range(max(TT, KH) + 3):
            if 3 <= t <= KH + 2:
                emit_qqu(t - 3)
            if t < TT:
                emit_tileA_ln(t)
            if t < KH:
                emit_qld(t)
            if 1 <= t <= TT:
                emit_tileA_xn(t - 1)
            if 1 <= t <= KH:
                emit_qfold(t - 1)
            if 2 <= t <= TT + 1:
                emit_tileA_trv(t - 2)
                if (t - 2) % 4 == 3:
                    emit_axp((t - 2) // 4)
            if 1 <= t <= TT:
                emit_tileA_tr(t - 1)
            if 1 <= t <= KH:
                emit_qtr1(t - 1)
            if 2 <= t <= KH + 1:
                emit_qred(t - 2)
                emit_qtrs(t - 2)
                emit_qcps(t - 2)
                emit_qbc(t - 2)
        M["phA_ve"] = ve.n

        def cx1(vector):
            vwait(vector); ve.inc(vector.tensor_reduce(scl[:, 3:4], amx[:, 0:TT], AX, MAX))
            M["amx1"] = ve.n
        on_v(cx1)

        def cx2(tensor):
            tensor.wait_ge(ve.h, M["amx1"])
            pe.inc(tensor.transpose(psS[0][0:1, 0:128], scl[:, 3:4], ident[:]))
            M["amx2"] = pe.n
        on_t(cx2)

        def cx3(vector):
            vector.wait_ge(pe.h, M["amx2"])
            vwait(vector); ve.inc(vector.memset(ccs[0:1, 0:8], 0.0))
            vwait(vector); ve.inc(vector.tensor_reduce(ccs[0:1, 0:1], psS[0][0:1, 0:128], AX, MAX))
            M["amx3"] = ve.n
        on_v(cx3)

        def cx4(scalar):
            scalar.wait_ge(ve.h, M["amx3"])
            scalar.dma_start(out=ccx_i.ap(), in_=ccs[0:1, 0:8]).then_inc(dCC.h, 16)
            dCC.n += 16
            M["ccx_up"] = dCC.n
        on_s(cx4)

        def cx5(gpsimd):
            gpsimd.wait_ge(dCC.h, M["ccx_up"])
            ge.inc(gpsimd.collective_compute(
                "AllReduce", MAX, replica_groups=[list(range(N_CORES))],
                ins=[ccx_i.ap().opt()], outs=[ccx_o.ap().opt()]))
            M["cc1"] = ge.n
        on_g(cx5)

        def scan(vector):
            vector.wait_ge(se.h, M["acol"])
            vector.wait_ge(pe.h, M[f"axp{GB-1}"])
            vwait(vector); ve.inc(vector.tensor_copy(bwt[:, 0:D], hs[:, 0:D]))
            M["dbc"] = ve.n
            for b in range(SEQ_PER_CORE):
                vwait(vector); ve.inc(vector.tensor_tensor_scan(
                    hs[:, b * L:(b + 1) * L],
                    acol[:, 0:1].broadcast_to([128, L]),
                    xproj[:, b * L:(b + 1) * L],
                    0.0, MULT, ADD))
            M["scan"] = ve.n
        on_v(scan)


        def cx6(scalar):
            scalar.wait_ge(ge.h, M["cc1"])
            scalar.dma_start(out=ccs[0:1, 8:16], in_=ccx_o.ap()).then_inc(dCC.h, 16)
            dCC.n += 16
            M["ccx_dn"] = dCC.n
        on_s(cx6)

        def cx7(tensor):
            tensor.wait_ge(dCC.h, M["ccx_dn"])
            pe.inc(tensor.matmul(psS[0][:, 0:1], ones[0:1, :], ccs[0:1, 8:9],
                                 start=True, stop=True))
            M["sxb"] = pe.n
        on_t(cx7)

        def cx8(vector):
            vector.wait_ge(pe.h, M["sxb"])
            vwait(vector); ve.inc(vector.reciprocal(scl[:, 0:1], psS[0][:, 0:1]))
            vwait(vector); ve.inc(vector.tensor_scalar(scl[:, 0:1], scl[:, 0:1],
                                        32767.0, None, MULT))
            vwait(vector); ve.inc(vector.tensor_scalar(dv[:, 0:KH], sj[:, 0:KH],
                                        scl[:, 0:1], None, MULT))
            vwait(vector); ve.inc(vector.reciprocal(dv[:, 0:KH], dv[:, 0:KH]))
            M["sxd"] = ve.n
        on_v(cx8)

        def bol(sync):
            sync.wait_ge(ve.h, M["dstash"])
            sync.dma_start(out=rowb[0:1, :], in_=bo_in).then_inc(dW.h, 16)
            dW.n += 16
            M["bol"] = dW.n
        on_y(bol)

        def emit_woq(i):
            def wol(scalar, i=i):
                if i <= 1:
                    scalar.wait_ge(ve.h, M[f"qqu{KH-1}"])
                else:
                    scalar.wait_ge(ve.h, M[f"woq{i-2}"])
                wbuf = qin[:, (i % 2) * D:(i % 2 + 1) * D]
                scalar.dma_start(out=wbuf,
                                 in_=wo_in[i * 128:(i + 1) * 128, :]
                                 ).then_inc(dW.h, 16)
                dW.n += 16
                M[f"wol{i}"] = dW.n
            on_s(wol)

            def woq(vector, i=i):
                vector.wait_ge(dW.h, M[f"wol{i}"])
                wbuf = qin[:, (i % 2) * D:(i % 2 + 1) * D]
                vwait(vector); ve.inc(vector.tensor_reduce(si[:, i:i + 1], wbuf, AX, MAX,
                                            apply_absolute_value=True))
                vwait(vector); ve.inc(vector.reciprocal(scl[:, 4:5], si[:, i:i + 1]))
                vwait(vector); ve.inc(vector.tensor_scalar(si[:, i:i + 1], scl[:, 4:5],
                                            127.0, None, MULT))
                vwait(vector); ve.inc(vector.tensor_scalar(wbuf, wbuf,
                                            si[:, i:i + 1], CM, MULT, ADD))
                vwait(vector); ve.inc(vector.tensor_scalar(woi[:, i * D:(i + 1) * D],
                                            wbuf, CM, None, SUB))
                M[f"woq{i}"] = ve.n
            on_v(woq)

        def emit_pieceB(g, k):
            def bnl(sync, g=g, k=k):
                if g == 0 and k == 0:
                    sync.wait_ge(ve.h, M[f"axn{TT-1}"])
                sync.wait_ge(dXN.h, M[f"axsp{g}"])
                prev = (g, k - 4) if k >= 4 else ((g - 1, KD - 4 + k) if g > 0 else None)
                if prev is not None:
                    sync.wait_ge(ve.h, M[f"blb{prev[0]}_{prev[1]}"])
                s = dXB[k % 4]
                sync.dma_start(out=xbuf[:, (k % 4) * 512:(k % 4 + 1) * 512],
                               in_=xnt_dram[g, :, k * 512:(k + 1) * 512]
                               ).then_inc(s.h, 16)
                s.n += 16
                M[f"bnl{g}_{k}"] = s.n
            on_y(bnl)

            def blb(vector, g=g, k=k):
                vector.wait_ge(dXB[k % 4].h, M[f"bnl{g}_{k}"])
                if k == 0:
                    if g <= 1:
                        vector.wait_ge(ve.h, M["sxd"])
                        vector.wait_ge(ve.h, M["scan"])
                    if g >= 2:
                        vector.wait_ge(pe.h, M[f"bmm{g-2}"])
                mh = movh if g % 2 == 0 else movh2
                xb = xbuf[:, (k % 4) * 512:(k % 4 + 1) * 512]
                xt = xtmp[:, (k % 2) * 512:(k % 2 + 1) * 512]
                hi = mh[:, k * 512:(k + 1) * 512]
                lo = mh[:, 4096 + k * 512:4096 + (k + 1) * 512]
                vwait(vector); ve.inc(vector.tensor_scalar(xt, xb, scl[:, 0:1],
                                            CM, MULT, ADD))
                vwait(vector); ve.inc(vector.tensor_scalar(xt, xt, CM, None, SUB))
                vwait(vector); ve.inc(vector.tensor_copy(hi, xt))
                vwait(vector); ve.inc(vector.tensor_tensor(lo, xt, hi, SUB))
                M[f"blb{g}_{k}"] = ve.n
            on_v(blb)

        def emit_bj(g, j):
            def bmm(tensor, g=g, j=j):
                if j == 0:
                    tensor.wait_ge(ve.h, M[f"blb{g}_{KD-1}"])
                if g == 0:
                    tensor.wait_ge(ve.h, M[f"qqu{j}"])
                prevj = (g, j - 4) if j >= 4 else ((g - 1, KH - 4 + j) if g > 0 else None)
                if prevj is not None:
                    tensor.wait_ge(se.h, M[f"bgl{prevj[0]}_{prevj[1]}"])
                    tensor.wait_ge(ve.h, M[f"brd{prevj[0]}_{prevj[1]}"])
                ps = psS[j % 4]
                mh = movh if g % 2 == 0 else movh2
                for k in range(KD):
                    w = wii[:, k * H + j * 128:k * H + j * 128 + 128]
                    pe.inc(tensor.matmul(ps[:], w, mh[:, k * 512:(k + 1) * 512],
                                         start=(k == 0), stop=False))
                    pe.inc(tensor.matmul(
                        ps[:], w, mh[:, 4096 + k * 512:4096 + (k + 1) * 512],
                        start=False, stop=(k == KD - 1)))
                M[f"bmmj{g}_{j}"] = pe.n
                if j == KH - 1:
                    M[f"bmm{g}"] = pe.n
            on_t(bmm)

            def bgl(scalar, g=g, j=j):
                scalar.wait_ge(pe.h, M[f"bmmj{g}_{j}"])
                if g == 0:
                    scalar.wait_ge(ve.h, M["sxd"])
                    if j < 1:
                        scalar.wait_ge(pe.h, M[f"qtr1_{KH-1}"])
                prevs = (g, j - 4) if j >= 4 else ((g - 1, KH - 4 + j) if g > 0 else None)
                if prevs is not None:
                    scalar.wait_ge(dSP[j % 4].h, M[f"bsp{prevs[0]}_{prevs[1]}"])
                sl = hslab[:, (j % 4) * 512:(j % 4 + 1) * 512]
                swait(scalar); se.inc(scalar.activation(sl, psS[j % 4][:], AF.Gelu,
                                         bias=bint[:, j:j + 1],
                                         scale=dv[:, j:j + 1]))
                M[f"bgl{g}_{j}"] = se.n
            on_s(bgl)

            def bspill(sync, g=g, j=j):
                sync.wait_ge(se.h, M[f"bgl{g}_{j}"])
                sl = hslab[:, (j % 4) * 512:(j % 4 + 1) * 512]
                s = dSP[j % 4]
                sync.dma_start(out=hid_dram[j, g, :, :], in_=sl).then_inc(s.h, 16)
                s.n += 16
                M[f"bsp{g}_{j}"] = s.n
            on_s(bspill)

            def brd(vector, g=g, j=j):
                vector.wait_ge(se.h, M[f"bgl{g}_{j}"])
                sl = hslab[:, (j % 4) * 512:(j % 4 + 1) * 512]
                vwait(vector); ve.inc(vector.tensor_reduce(amhg[:, j:j + 1], sl, AX, MAX,
                                            apply_absolute_value=True))
                M[f"brd{g}_{j}"] = ve.n
            on_v(brd)

        for g in range(GB):
            if g == 0:
                for k in range(KD):
                    emit_pieceB(0, k)
            for j in range(KH):
                emit_bj(g, j)
                if g + 1 < GB and j >= 8 and (j - 8) % 2 == 0 and (j - 8) // 2 < KD:
                    emit_pieceB(g + 1, (j - 8) // 2)
                if GB > 1 and g == 1:
                    emit_woq(j)

            def bam(vector, g=g):
                if g == 0:
                    vwait(vector); ve.inc(vector.tensor_copy(amh[:, 0:KH], amhg[:, 0:KH]))
                else:
                    vwait(vector); ve.inc(vector.tensor_tensor(amh[:, 0:KH], amh[:, 0:KH],
                                                amhg[:, 0:KH], MAX))
                M[f"bam{g}"] = ve.n
            on_v(bam)
            if GB == 1 and g == 0:
                for j in range(KH):
                    emit_woq(j)
        M["woq_all"] = M[f"woq{KH-1}"]

        def ch1(vector):
            vwait(vector); ve.inc(vector.tensor_reduce(scl[:, 5:6], amh[:, 0:KH], AX, MAX))
            M["amh1"] = ve.n
        on_v(ch1)

        def ch2(tensor):
            tensor.wait_ge(ve.h, M["amh1"])
            pe.inc(tensor.transpose(psS[0][0:1, 0:128], scl[:, 5:6], ident[:]))
            M["amh2"] = pe.n
        on_t(ch2)

        def ch3(vector):
            vector.wait_ge(pe.h, M["amh2"])
            vwait(vector); ve.inc(vector.memset(ccs[0:1, 0:8], 0.0))
            vwait(vector); ve.inc(vector.tensor_reduce(ccs[0:1, 0:1], psS[0][0:1, 0:128], AX, MAX))
            M["amh3"] = ve.n
        on_v(ch3)

        def ch4(scalar):
            scalar.wait_ge(ve.h, M["amh3"])
            scalar.dma_start(out=cch_i.ap(), in_=ccs[0:1, 0:8]).then_inc(dCC.h, 16)
            dCC.n += 16
            M["cch_up"] = dCC.n
        on_s(ch4)

        def ch5(gpsimd):
            gpsimd.wait_ge(dCC.h, M["cch_up"])
            ge.inc(gpsimd.collective_compute(
                "AllReduce", MAX, replica_groups=[list(range(N_CORES))],
                ins=[cch_i.ap().opt()], outs=[cch_o.ap().opt()]))
            M["cc2"] = ge.n
        on_g(ch5)

        def ch6(scalar):
            scalar.wait_ge(ge.h, M["cc2"])
            scalar.dma_start(out=ccs[0:1, 8:16], in_=cch_o.ap()).then_inc(dCC.h, 16)
            dCC.n += 16
            M["cch_dn"] = dCC.n
        on_s(ch6)

        def ch7(tensor):
            tensor.wait_ge(dCC.h, M["cch_dn"])
            pe.inc(tensor.matmul(psS[0][:, 0:1], ones[0:1, :], ccs[0:1, 8:9],
                                 start=True, stop=True))
            M["shb"] = pe.n
        on_t(ch7)

        def ch8(vector):
            vector.wait_ge(pe.h, M["shb"])
            vwait(vector); ve.inc(vector.reciprocal(scl[:, 1:2], psS[0][:, 0:1]))
            vwait(vector); ve.inc(vector.tensor_scalar(scl[:, 1:2], scl[:, 1:2],
                                        32767.0, None, MULT))
            vwait(vector); ve.inc(vector.tensor_scalar(civ[:, 0:KH], si[:, 0:KH],
                                        scl[:, 1:2], None, MULT))
            vwait(vector); ve.inc(vector.reciprocal(civ[:, 0:KH], civ[:, 0:KH]))
            M["shd"] = ve.n
        on_v(ch8)

        def dtile(g, sub, nh):
            if g % 2 == 0:
                return psB[sub][:, nh * 512:(nh + 1) * 512]
            return psS[sub * 2 + nh][:, 0:512]

        def dpre(tensor):
            tensor.wait_ge(ve.h, M["scan"])
            tensor.wait_ge(dW.h, M["bol"])
            for sub in range(2):
                stat = hs[:, sub * 128:sub * 128 + 128]
                for nh in range(2):
                    pe.inc(tensor.matmul(
                        psB[sub][:, nh * 512:(nh + 1) * 512],
                        stat, cwt[:, nh * 512:(nh + 1) * 512],
                        start=True, stop=False))
                for nh in range(2):
                    pe.inc(tensor.matmul(
                        psB[sub][:, nh * 512:(nh + 1) * 512],
                        ones[0:1, :], rowb[0:1, nh * 512:(nh + 1) * 512],
                        start=False, stop=False))
        on_t(dpre)

        deferred = []
        for g in range(GD):
            tok0 = g * 256
            gB, hB = tok0 // 512, (tok0 % 512) // 256

            def dxl(scalar, g=g, tok0=tok0):
                if g == 0:
                    scalar.wait_ge(ve.h, M[f"blb{GB-1}_{KD-1}"])
                else:
                    scalar.wait_ge(ve.h, M[f"dev{g-1}"])
                scalar.dma_start(
                    out=xbuf[:, 0:2 * D].rearrange("p (s d) -> p s d", s=2),
                    in_=x_in[tok0:tok0 + 256, :].rearrange("(s p) d -> p s d", s=2),
                ).then_inc(dXD.h, 16)
                dXD.n += 16
                M[f"dxl{g}"] = dXD.n

            if g == 0:
                on_s(dxl)

            pend_mm = []
            for i in range(KH):
                sl = i % 2
                if i == 2:
                    if deferred:
                        deferred[0]()
                    for fn in pend_mm:
                        on_t(fn)
                    pend_mm = []
                if i == 3 and deferred:
                    deferred[1]()
                if i == min(5, KH - 3) and deferred:
                    deferred[2]()
                if i == min(7, KH - 2) and deferred:
                    deferred[3]()
                    deferred = []
                    on_s(dxl)

                def dsr(sync, g=g, gB=gB, hB=hB, i=i):
                    if g == 0 and i < 8:
                        sync.wait_ge(ve.h, M["woq_all"])
                    sync.wait_ge(dSP[i % 4].h, M[f"bsp{gB}_{i}"])
                    prev = (g, i - 8) if i >= 8 else ((g - 1, KH - 8 + i) if g > 0 else None)
                    if prev is not None:
                        sync.wait_ge(ve.h, M[f"dp{prev[0]}_{prev[1]}"])
                    s = dSR[i % 8]
                    sync.dma_start(
                        out=qin[:, (i % 8) * 256:(i % 8) * 256 + 256],
                        in_=hid_dram[i, gB, :, hB * 256:(hB + 1) * 256],
                    ).then_inc(s.h, 16)
                    s.n += 16
                    M[f"dsr{g}_{i}"] = s.n
                on_s(dsr)

                def dp_pair(vector, g=g, i0=i - 1):
                    q = (i0 // 2) % 2
                    vector.wait_ge(dSR[i0 % 8].h, M[f"dsr{g}_{i0}"])
                    vector.wait_ge(dSR[(i0 + 1) % 8].h, M[f"dsr{g}_{i0+1}"])
                    if g == 0 and i0 == 0:
                        vector.wait_ge(ve.h, M["shd"])
                        for s4 in range(4):
                            vector.wait_ge(dSP[s4].h, dSP[s4].n)
                    prev = ((g, i0 - 3) if i0 >= 4
                            else ((g - 1, KH - 3 + i0) if g > 0 else None))
                    if prev is not None:
                        vector.wait_ge(pe.h, M[f"dmm{prev[0]}_{prev[1]}"])
                    buf2 = qin[:, (i0 % 8) * 256:(i0 % 8) * 256 + 512]
                    p = pF(q)
                    vwait(vector); ve.inc(vector.tensor_scalar(p, buf2, scl[:, 1:2],
                                                CM, MULT, ADD))
                    M[f"dp{g}_{i0}"] = ve.n
                    M[f"dp{g}_{i0+1}"] = ve.n
                    vwait(vector); ve.inc(vector.tensor_scalar(p[:, 0:256], p[:, 0:256],
                                                CM, civ[:, i0:i0 + 1], SUB, MULT))
                    vwait(vector); ve.inc(vector.tensor_scalar(p[:, 256:512], p[:, 256:512],
                                                CM, civ[:, i0 + 1:i0 + 2], SUB, MULT))
                    vwait(vector); ve.inc(vector.tensor_copy(plbp(q, 0), p))
                    vwait(vector); ve.inc(vector.tensor_tensor(plbp(q, 1), p, plbp(q, 0), SUB))
                    M[f"dpl{g}_{i0}"] = ve.n
                    M[f"dpl{g}_{i0+1}"] = ve.n
                if i % 2 == 1:
                    on_v(dp_pair)

                def mk_dmm(ii, g=g, tok0=tok0):
                  def dmm(tensor, g=g, i=ii, tok0=tok0):
                    tensor.wait_ge(ve.h, M[f"dpl{g}_{i}"])
                    if i == 0 and g >= 1:
                        if g >= 2:
                            tensor.wait_ge(ve.h, M[f"devc{g-2}"])
                        else:
                            tensor.wait_ge(se.h, M[f"bgl{GB-1}_{KH-1}"])
                        for sub in range(2):
                            stat = hs[:, tok0 + sub * 128:tok0 + sub * 128 + 128]
                            for nh in range(2):
                                pe.inc(tensor.matmul(
                                    dtile(g, sub, nh),
                                    stat, cwt[:, nh * 512:(nh + 1) * 512],
                                    start=(nh >= 0), stop=False))
                            for nh in range(2):
                                pe.inc(tensor.matmul(
                                    dtile(g, sub, nh),
                                    ones[0:1, :], rowb[0:1, nh * 512:(nh + 1) * 512],
                                    start=False, stop=False))
                    q, io = (i // 2) % 2, i % 2
                    for sub in range(2):
                        for lb in range(2):
                            stat = plbp(q, lb)[:, io * 256 + sub * 128:
                                               io * 256 + sub * 128 + 128]
                            for nh in range(2):
                                pe.inc(tensor.matmul(
                                    dtile(g, sub, nh),
                                    stat,
                                    woi[:, i * D + nh * 512:i * D + (nh + 1) * 512],
                                    start=False, stop=(i == KH - 1 and lb == 1)))
                    M[f"dmm{g}_{i}"] = pe.n
                  return dmm
                if i % 2 == 1:
                    for ii in (i - 1, i):
                        if ii < 2:
                            pend_mm.append(mk_dmm(ii))
                        else:
                            on_t(mk_dmm(ii))
            M[f"dy{g}"] = M[f"dmm{g}_{KH-1}"]

            def mk_tail(g=g, tok0=tok0):
                def emit_devc():
                    def devc(vector, g=g):
                        vector.wait_ge(pe.h, M[f"dy{g}"])
                        if g == 0:
                            vector.wait_ge(pe.h, M[f"bmm{GB-1}"])
                        if g >= 1:
                            vector.wait_ge(dOUT.h, M[f"dout{g-1}"])
                        for sub in range(2):
                            ot = outst[:, sub * D:(sub + 1) * D]
                            if g % 2 == 0:
                                vwait(vector); ve.inc(vector.tensor_copy(ot, psB[sub][:]))
                            else:
                                for nh in range(2):
                                    vwait(vector); ve.inc(vector.tensor_copy(
                                        ot[:, nh * 512:(nh + 1) * 512],
                                        dtile(g, sub, nh)))
                        M[f"devc{g}"] = ve.n
                    on_v(devc)

                def mk_dev(sub):
                    def emit_dev():
                        def dev(vector, g=g, sub=sub):
                            if sub == 0:
                                vector.wait_ge(dXD.h, M[f"dxl{g}"])
                            t = g * 2 + sub
                            xt = xbuf[:, sub * D:(sub + 1) * D]
                            ot = outst[:, sub * D:(sub + 1) * D]
                            vwait(vector); ve.inc(vector.tensor_scalar(
                                xtmp[:, 0:D], xt, mu[:, t:t + 1],
                                rsig[:, t:t + 1], SUB, MULT))
                            vwait(vector); ve.inc(vector.tensor_tensor(
                                xtmp[:, 0:D], xtmp[:, 0:D], bwt[:, 0:D], MULT))
                            vwait(vector); ve.inc(vector.tensor_tensor(
                                xtmp[:, 0:D], xtmp[:, 0:D], xt, ADD))
                            vwait(vector); ve.inc(vector.tensor_tensor(
                                ot, ot, xtmp[:, 0:D], ADD))
                            if sub == 1:
                                M[f"dev{g}"] = ve.n
                        on_v(dev)
                    return emit_dev

                def emit_dout():
                    def dout(sync, g=g, tok0=tok0):
                        sync.wait_ge(ve.h, M[f"dev{g}"])
                        sync.dma_start(
                            out=y_out[tok0:tok0 + 256, :].rearrange("(s p) d -> p s d", s=2),
                            in_=outst[:, 0:2 * D].rearrange("p (s d) -> p s d", s=2),
                        ).then_inc(dOUT.h, 16)
                        dOUT.n += 16
                        M[f"dout{g}"] = dOUT.n
                    on_y(dout)
                return [emit_devc, mk_dev(0), mk_dev(1), emit_dout]
            deferred = mk_tail()

        for fn in deferred:
            fn()
        deferred = []

        if debug:
            def dbg(sync):
                sync.wait_ge(ve.h, ve.n)
                sync.wait_ge(se.h, se.n)
                sync.wait_ge(pe.h, pe.n)
                for src_ap, c0 in [(mu, 0), (rsig, TT), (rss, 2 * TT), (amx, 3 * TT),
                                   (sj, 4 * TT), (si, 4 * TT + KH), (dv, 4 * TT + 2 * KH),
                                   (civ, 4 * TT + 3 * KH)]:
                    w = src_ap.ap().shape[1]
                    sync.dma_start(out=dbg_small[:, c0:c0 + w], in_=src_ap[:, 0:w]
                                   ).then_inc(dOUT.h, 16)
                    dOUT.n += 16
                sync.dma_start(out=dbg_small[:, 248:252].rearrange("p (a b) -> p a b", a=4),
                               in_=scl[:, 0:4].rearrange("p (a b) -> p a b", a=4),
                               allow_slow_non_contiguous=True).then_inc(dOUT.h, 16)
                sync.dma_start(out=dbg_small[:, 252:253], in_=acol[:, 0:1],
                               allow_slow_non_contiguous=True).then_inc(dOUT.h, 16)
                dOUT.n += 32
                sync.dma_start(out=dbg_xproj, in_=xproj[:, 0:T]).then_inc(dOUT.h, 16)
                sync.dma_start(out=dbg_hs, in_=hs[:, 0:T]).then_inc(dOUT.h, 16)
                sync.dma_start(out=dbg_wii, in_=wii[:]).then_inc(dOUT.h, 16)
                sync.dma_start(out=dbg_woi, in_=woi[:]).then_inc(dOUT.h, 16)
                sync.dma_start(out=dbg_movb, in_=movb[:]).then_inc(dOUT.h, 16)
                dOUT.n += 80
            on_y(dbg)

        def fin(sync):
            sync.wait_ge(dOUT.h, dOUT.n)
        on_y(fin)

    return nc


def kernel(L0_attn_x, L0_attn_B_w, L0_attn_C_w, L0_attn_log_A, L0_attn_D_param,
           L0_ffn_W_in, L0_ffn_b_in, L0_ffn_W_out, L0_ffn_b_out):
    cfg = Cfg()
    B, L, D = L0_attn_x.shape
    assert (B, L, D) == (16, 2048, 1024)
    x = np.ascontiguousarray(np.asarray(L0_attn_x, dtype=np.float32))
    shards = [np.ascontiguousarray(
        x[c * SEQ_PER_CORE:(c + 1) * SEQ_PER_CORE].reshape(cfg.T, D))
        for c in range(N_CORES)]
    common = {
        "bw": np.ascontiguousarray(np.asarray(L0_attn_B_w, np.float32)),
        "cw": np.ascontiguousarray(np.asarray(L0_attn_C_w, np.float32)),
        "la": np.ascontiguousarray(np.asarray(L0_attn_log_A, np.float32).reshape(1, 128)),
        "dp": np.ascontiguousarray(np.asarray(L0_attn_D_param, np.float32).reshape(1, D)),
        "wi": np.ascontiguousarray(np.asarray(L0_ffn_W_in, np.float32)),
        "bi": np.ascontiguousarray(np.asarray(L0_ffn_b_in, np.float32).reshape(cfg.KH, 128)),
        "wo": np.ascontiguousarray(np.asarray(L0_ffn_W_out, np.float32)),
        "bo": np.ascontiguousarray(np.asarray(L0_ffn_b_out, np.float32).reshape(1, D)),
        "ident": np.eye(128, dtype=np.float32),
    }
    nc = build(cfg)
    in_maps = [dict(common, x=s) for s in shards]
    res = run_bass_kernel_spmd(nc, in_maps, core_ids=list(range(N_CORES)))
    out = np.concatenate([np.asarray(r["y"]).reshape(SEQ_PER_CORE, L, D)
                          for r in res.results], axis=0)
    return out


# version 65
# speedup vs baseline: 1.0230x; 1.0230x over previous
import numpy as np
from contextlib import ExitStack

import concourse.bass as bass
import concourse.mybir as mybir
from concourse.alu_op_type import AluOpType
from concourse.bass_utils import run_bass_kernel_spmd

F32 = mybir.dt.float32
BF16 = mybir.dt.bfloat16
F16 = mybir.dt.float16
AF = mybir.ActivationFunctionType
AX = mybir.AxisListType.X
MULT = AluOpType.mult
ADD = AluOpType.add
SUB = AluOpType.subtract
MAX = AluOpType.max
MAXOP = AluOpType.max
MINOP = AluOpType.min

CM = float(3 << 22)
EPS = 1e-5
N_CORES = 8
SEQ_PER_CORE = 2


class Cfg:
    def __init__(self, L=2048, D=1024, H=4096):
        self.L, self.D, self.H = L, D, H
        self.T = SEQ_PER_CORE * L
        self.KD = D // 128
        self.KH = H // 128
        self.TT = self.T // 128
        self.GB = self.T // 512
        self.GD = self.T // 256


class Sem:
    def __init__(self, st, nc, name):
        self.h = st.enter_context(nc.semaphore(name))
        self.n = 0

    def inc(self, instr, k=1):
        instr.then_inc(self.h, k)
        self.n += k
        return self.n


def build(cfg: Cfg, debug=False):
    nc = bass.Bass("TRN2", target_bir_lowering=False,
                   dynamic_dma_scratch_size=2048,
                   detect_race_conditions=False)
    L, D, H, T = cfg.L, cfg.D, cfg.H, cfg.T
    KD, KH, TT, GB, GD = cfg.KD, cfg.KH, cfg.TT, cfg.GB, cfg.GD

    x_in = nc.dram_tensor("x", [T, D], F32, kind="ExternalInput").ap()
    bw_in = nc.dram_tensor("bw", [128, D], F32, kind="ExternalInput").ap()
    cw_in = nc.dram_tensor("cw", [D, 128], F32, kind="ExternalInput").ap()
    la_in = nc.dram_tensor("la", [1, 128], F32, kind="ExternalInput").ap()
    dp_in = nc.dram_tensor("dp", [1, D], F32, kind="ExternalInput").ap()
    wi_in = nc.dram_tensor("wi", [D, H], F32, kind="ExternalInput").ap()
    bi_in = nc.dram_tensor("bi", [KH, 128], F32, kind="ExternalInput").ap()
    wo_in = nc.dram_tensor("wo", [H, D], F32, kind="ExternalInput").ap()
    bo_in = nc.dram_tensor("bo", [1, D], F32, kind="ExternalInput").ap()
    id_in = nc.dram_tensor("ident", [128, 128], F32, kind="ExternalInput").ap()
    y_out = nc.dram_tensor("y", [T, D], F32, kind="ExternalOutput").ap()

    xnt_dram = nc.dram_tensor("xnt_spill", [GB, 128, 4096], F32).ap()
    hid_dram = nc.dram_tensor("hid_spill", [KH, GB, 128, 512], F32,
                              kind="ExternalOutput" if debug else "Internal").ap()
    if debug:
        dbg_small = nc.dram_tensor("dbg_small", [128, 256], F32, kind="ExternalOutput").ap()
        dbg_xproj = nc.dram_tensor("dbg_xproj", [128, T], F32, kind="ExternalOutput").ap()
        dbg_hs = nc.dram_tensor("dbg_hs", [128, T], F32, kind="ExternalOutput").ap()
        dbg_wii = nc.dram_tensor("dbg_wii", [128, KD * H], BF16, kind="ExternalOutput").ap()
        dbg_woi = nc.dram_tensor("dbg_woi", [128, KH * D], BF16, kind="ExternalOutput").ap()
        dbg_movb = nc.dram_tensor("dbg_movb", [128, 4096], F32, kind="ExternalOutput").ap()
    ccx_i = nc.dram_tensor("ccx_i", [1, 8], F32)
    ccx_o = nc.dram_tensor("ccx_o", [1, 8], F32)
    cch_i = nc.dram_tensor("cch_i", [1, 8], F32)
    cch_o = nc.dram_tensor("cch_o", [1, 8], F32)

    with ExitStack() as st:
        E = st.enter_context

        def sb(name, p, f, dt=F32):
            return E(nc.sbuf_tensor(name, [p, f], dt))

        ident = sb("identt", 128, 128)
        ones = sb("ones", 1, 128)
        srow = sb("srow", 1, 256)
        arow = sb("arow", 1, 128)
        acol = sb("acol", 128, 1)
        epsc = sb("epsc", 128, 1)
        cmc = sb("cmc", 128, 1)
        cmn = sb("cmn", 128, 1)
        identb = sb("identb", 128, 128, BF16)
        cwt = sb("cwt", 128, D)
        bwt = sb("bwt", 128, D)
        bint = sb("bint", 128, KH)
        binrow = sb("binrow", KH, 128)
        rowb = sb("rowb", 1, D)
        mu = sb("mu", 128, TT)
        rsig = sb("rsig", 128, TT)
        rss = sb("rss", 128, TT)
        sj = sb("sj", 128, KH)
        si = sb("si", 128, KH)
        dv = sb("dv", 128, KH)
        civ = sb("civ", 128, KH)
        amx = sb("amx", 128, TT)
        amh = sb("amh", 128, KH)
        amhg = sb("amhg", 128, KH)
        scl = sb("scl", 128, 8)
        ccs = sb("ccs", 1, 16)
        bnst = sb("bnst", 128, 12)
        bnag = sb("bnag", 128, 2)
        xproj = sb("xproj", 128, max(T, 2 * D, 4096))
        hs = sb("hs", 128, max(T, D))
        wii = sb("wii", 128, KD * H, BF16)
        woi = sb("woi", 128, KH * D, BF16)
        movb = sb("movb", 128, 4096)
        xbuf = sb("xbuf", 128, 2 * D)
        xtmp = sb("xtmp", 128, D)
        qin = sb("qin", 128, 2 * D)
        hslab = sb("hslab", 128, 4 * 512)

        movh = movb.ap().bitcast(BF16)
        movh2 = xproj[:, 0:4096].bitcast(BF16)
        qin_lo = qin[:, 0:D]
        qt = qin[:, D:2 * D]
        outst = xproj[:, 0:2 * D]
        plF = hslab[:, 0:512]
        plR = hslab[:, 512:1024]
        plB = hslab[:, 1024:2048].bitcast(F16)

        def pF(q):
            return plF[:, 0:512] if q == 0 else plR[:, 0:512]

        def plbp(q, which):
            return plB[:, q * 1024 + which * 512:q * 1024 + (which + 1) * 512]

        psS = [E(nc.psum_tensor(f"psS{i}", [128, 512], F32)) for i in range(4)]
        psB = [E(nc.psum_tensor(f"psB{i}", [128, 1024], F32)) for i in range(2)]

        dX = [Sem(st, nc, f"dX{i}") for i in range(2)]
        dXB = [Sem(st, nc, f"dXB{i}") for i in range(4)]
        dXN = Sem(st, nc, "dXN")
        dXD = Sem(st, nc, "dXD")
        dW = Sem(st, nc, "dW")
        dSP = [Sem(st, nc, f"dSP{i}") for i in range(4)]
        dSR = [Sem(st, nc, f"dSR{i}") for i in range(8)]
        dOUT = Sem(st, nc, "dOUT")
        dCC = Sem(st, nc, "dCC")
        pe = Sem(st, nc, "pe")
        ve = Sem(st, nc, "ve")
        se = Sem(st, nc, "se")
        ge = Sem(st, nc, "ge")

        block = E(nc.Block())
        M = {}
        on_t, on_v, on_s = block.tensor, block.vector, block.scalar
        on_y, on_g = block.sync, block.gpsimd

        def vwait(v):
            v.wait_ge(ve.h, ve.n)

        def swait(s):
            s.wait_ge(se.h, se.n)

        def ld0(sync):
            sync.dma_start(out=ident[:], in_=id_in).then_inc(dW.h, 16)
            sync.dma_start(out=arow[0:1, :], in_=la_in).then_inc(dW.h, 16)
            sync.dma_start(out=binrow[0:KH, :], in_=bi_in).then_inc(dW.h, 16)
            sync.dma_start(out=rowb[0:1, :], in_=dp_in).then_inc(dW.h, 16)
            for k in range(KD):
                sync.dma_start(out=qin[:, k * 128:(k + 1) * 128],
                               in_=cw_in[k * 128:(k + 1) * 128, :]).then_inc(dW.h, 16)
            sync.dma_start(out=xtmp[:, 0:D], in_=bw_in).then_inc(dW.h, 16)
            dW.n += 16 * (5 + KD)
            M["ld0"] = dW.n
        on_y(ld0)

        def c0v(vector):
            vector.wait_ge(dW.h, M["ld0"])
            vwait(vector); ve.inc(vector.memset(ones[:], 1.0))
            vwait(vector); ve.inc(vector.memset(epsc[:], EPS))
            vwait(vector); ve.inc(vector.memset(cmc[:], CM))
            vwait(vector); ve.inc(vector.memset(cmn[:], -CM))
            vwait(vector); ve.inc(vector.tensor_copy(identb[:], ident[:]))
            M["ones"] = ve.n
        on_v(c0v)

        def c0t(tensor):
            tensor.wait_ge(dW.h, M["ld0"])
            tensor.wait_ge(ve.h, M["ones"])
            pe.inc(tensor.transpose(psS[0][:, 0:1], arow[0:1, :], ident[0:1, 0:1]))
            pe.inc(tensor.transpose(psS[1][:, 0:KH], binrow[0:KH, :], ident[0:KH, 0:KH]))
            M["tr_la"] = pe.n
            for h in range(2):
                pe.inc(tensor.matmul(psS[2 + h][:, 0:512], ones[0:1, :],
                                     rowb[0:1, h * 512:(h + 1) * 512],
                                     start=True, stop=True))
            M["dpb"] = pe.n
        on_t(c0t)

        def c0v2(vector):
            vector.wait_ge(pe.h, M["tr_la"])
            vwait(vector); ve.inc(vector.tensor_copy(acol[:, 0:1], psS[0][:, 0:1]))
            vwait(vector); ve.inc(vector.tensor_copy(bint[:, 0:KH], psS[1][:, 0:KH]))
            M["bint"] = ve.n
            vector.wait_ge(pe.h, M["dpb"])
            for h in range(2):
                vwait(vector); ve.inc(vector.tensor_copy(hs[:, h * 512:(h + 1) * 512],
                                          psS[2 + h][:, 0:512]))
            M["dstash"] = ve.n
        on_v(c0v2)

        def c0s(scalar):
            scalar.wait_ge(ve.h, M["bint"])
            swait(scalar); se.inc(scalar.activation(acol[:, 0:1], acol[:, 0:1], AF.Exp))
            M["acol"] = se.n
        on_s(c0s)

        for k in range(KD):
            def cwk(tensor, k=k):
                tensor.wait_ge(ve.h, M["bint"] if k == 0 else M[f"cwt{k-1}"])
                pe.inc(tensor.transpose(psS[0][:, 0:128],
                                        qin[:, k * 128:(k + 1) * 128], ident[:]))
                M[f"cwtr{k}"] = pe.n
            on_t(cwk)

            def cwkv(vector, k=k):
                vector.wait_ge(pe.h, M[f"cwtr{k}"])
                vwait(vector); ve.inc(vector.tensor_copy(cwt[:, k * 128:(k + 1) * 128],
                                          psS[0][:, 0:128]))
                M[f"cwt{k}"] = ve.n
            on_v(cwkv)

        for k in range(KD):
            def bwk(tensor, k=k):
                if k == 0:
                    tensor.wait_ge(ve.h, M["bint"])
                else:
                    tensor.wait_ge(se.h, M[f"bwt{k-1}"])
                pe.inc(tensor.transpose(psS[1][:, 0:128],
                                        xtmp[:, k * 128:(k + 1) * 128], ident[:]))
                M[f"bwtr{k}"] = pe.n
            on_t(bwk)

            def bwkv(scalar, k=k):
                scalar.wait_ge(pe.h, M[f"bwtr{k}"])
                swait(scalar); se.inc(scalar.copy(bwt[:, k * 128:(k + 1) * 128], psS[1][:, 0:128]))
                M[f"bwt{k}"] = se.n
            on_s(bwkv)
        M["bwt"] = se.n
        M["bwtr_last"] = None

        QF = hslab[:, 0:128]
        qslot3 = woi.ap().bitcast(F32)[:, 0:D]

        def qbuf(j):
            s = j % 3
            return qin[:, s * D:(s + 1) * D] if s < 2 else qslot3

        def emit_qld(j):
            def ql(scalar, j=j):
                if j <= 1:
                    scalar.wait_ge(pe.h, M[f"cwtr{KD-1}"])
                elif j >= 3:
                    scalar.wait_ge(ve.h, M[f"qqu{j-3}"])
                buf = qbuf(j)
                scalar.dma_start(
                    out=buf.rearrange("p (k c) -> p k c", k=KD),
                    in_=wi_in[:, j * 128:(j + 1) * 128].rearrange(
                        "(k p) c -> p k c", k=KD),
                ).then_inc(dW.h, 16)
                dW.n += 16
                M[f"qld{j}"] = dW.n
            on_s(ql)

        def emit_qfold(j):
            def qfold(vector, j=j):
                vector.wait_ge(dW.h, M[f"qld{j}"])
                if j >= 1:
                    vector.wait_ge(pe.h, M[f"qtr1_{j-1}"])
                buf = qbuf(j)
                in3 = buf.rearrange("p (k c) -> p c k", k=KD)
                vwait(vector); ve.inc(vector.tensor_reduce(QF[:, 0:128], in3, AX, MAX,
                                            apply_absolute_value=True))
                M[f"qfold{j}"] = ve.n
            on_v(qfold)

        def emit_qtr1(j):
            def qtr1(tensor, j=j):
                tensor.wait_ge(ve.h, M[f"qfold{j}"])
                if j == 0:
                    tensor.wait_ge(se.h, M["bwt"])
                pe.inc(tensor.transpose(psB[0][:, (j % 2) * 128:(j % 2 + 1) * 128],
                                        QF[:, 0:128], ident[:]))
                M[f"qtr1_{j}"] = pe.n
            on_t(qtr1)

        def emit_qred(j):
            def qred(vector, j=j):
                vector.wait_ge(pe.h, M[f"qtr1_{j}"])
                sl = psB[0][:, (j % 2) * 128:(j % 2 + 1) * 128]
                vwait(vector); ve.inc(vector.tensor_reduce(sj[:, j:j + 1], sl, AX, MAX))
                vwait(vector); ve.inc(vector.reciprocal(scl[:, 2:3], sj[:, j:j + 1]))
                vwait(vector); ve.inc(vector.tensor_scalar(sj[:, j:j + 1], scl[:, 2:3],
                                            127.0, None, MULT))
                M[f"qred{j}"] = ve.n
            on_v(qred)

        def emit_qtrs(j):
            def qtrs(tensor, j=j):
                tensor.wait_ge(ve.h, M[f"qred{j}"])
                pe.inc(tensor.transpose(
                    psB[0][0:1, 256 + (j % 2) * 128:384 + (j % 2) * 128],
                    sj[:, j:j + 1], ident[:]))
                M[f"qtrs{j}"] = pe.n
            on_t(qtrs)

        def emit_qcps(j):
            def qcps(vector, j=j):
                vector.wait_ge(pe.h, M[f"qtrs{j}"])
                vwait(vector); ve.inc(vector.tensor_copy(
                    srow[0:1, (j % 2) * 128:(j % 2 + 1) * 128],
                    psB[0][0:1, 256 + (j % 2) * 128:384 + (j % 2) * 128]))
                M[f"qcps{j}"] = ve.n
            on_v(qcps)

        def emit_qbc(j):
            def qbc(tensor, j=j):
                tensor.wait_ge(ve.h, M[f"qcps{j}"])
                pe.inc(tensor.matmul(
                    psB[0][:, 512 + (j % 2) * 128:640 + (j % 2) * 128],
                    ones[0:1, :], srow[0:1, (j % 2) * 128:(j % 2 + 1) * 128],
                    start=True, stop=True))
                M[f"qbc{j}"] = pe.n
            on_t(qbc)

        def emit_qqu(j):
            def qqu(vector, j=j):
                vector.wait_ge(pe.h, M[f"qbc{j}"])
                buf = qbuf(j)
                b3 = buf.rearrange("p (k c) -> p k c", k=KD)
                sbc3 = psB[0][:, 512 + (j % 2) * 128:640 + (j % 2) * 128].rearrange(
                    "p (k c) -> p k c", k=1).broadcast_to([128, KD, 128])
                dst = wii.ap().rearrange("p (k h) -> p k h", k=KD)[
                    :, :, j * 128:(j + 1) * 128]
                vwait(vector); ve.inc(vector.tensor_tensor(b3, b3, sbc3, MULT))
                vwait(vector); ve.inc(vector.tensor_scalar(dst, b3, CM, CM, ADD, SUB))
                M[f"qqu{j}"] = ve.n
            on_v(qqu)


        def emit_tileA_ln(t):
            def axl(sync, t=t):
                if t >= 2:
                    sync.wait_ge(ve.h, M[f"axn{t-2}"])
                s = dX[t % 2]
                sync.dma_start(out=xbuf[:, (t % 2) * D:(t % 2 + 1) * D],
                               in_=x_in[t * 128:(t + 1) * 128, :]).then_inc(s.h, 16)
                s.n += 16
                M[f"axl{t}"] = s.n
            on_y(axl)

            def aln(vector, t=t):
                vector.wait_ge(dX[t % 2].h, M[f"axl{t}"])
                xt = xbuf[:, (t % 2) * D:(t % 2 + 1) * D]
                for cdx in range(D // 512):
                    vwait(vector); ve.inc(vector.bn_stats(bnst[:, 6 * cdx:6 * cdx + 6],
                                           xt[:, cdx * 512:(cdx + 1) * 512]))
                vwait(vector); ve.inc(vector.bn_aggr(bnag[:], bnst[:, 0:6 * (D // 512)]))
                vwait(vector); ve.inc(vector.tensor_copy(mu[:, t:t + 1], bnag[:, 0:1]))
                vwait(vector); ve.inc(vector.tensor_copy(rsig[:, t:t + 1], bnag[:, 1:2]))
                M[f"avar{t}"] = ve.n
            on_v(aln)

            def asd(scalar, t=t):
                scalar.wait_ge(ve.h, M[f"avar{t}"])
                swait(scalar); se.inc(scalar.activation(rsig[:, t:t + 1], rsig[:, t:t + 1],
                                         AF.Sqrt, bias=epsc[:, 0:1]))
                M[f"asd{t}"] = se.n
            on_s(asd)

        def emit_tileA_xn(t):
            def axn(vector, t=t):
                vector.wait_ge(se.h, M[f"asd{t}"])
                if t == 0:
                    vector.wait_ge(pe.h, M[f"bwtr{KD-1}"])
                else:
                    vector.wait_ge(pe.h, M[f"atr{t-1}_1"])
                xt = xbuf[:, (t % 2) * D:(t % 2 + 1) * D]
                vwait(vector); ve.inc(vector.reciprocal(rsig[:, t:t + 1], rsig[:, t:t + 1]))
                vwait(vector); ve.inc(vector.tensor_scalar(xtmp[:, 0:D], xt, mu[:, t:t + 1],
                                            rsig[:, t:t + 1], SUB, MULT))
                vwait(vector); ve.inc(vector.tensor_reduce(amx[:, t:t + 1], xtmp[:, 0:D],
                                            AX, MAX, apply_absolute_value=True))
                M[f"axn{t}"] = ve.n
            on_v(axn)

        def emit_tileA_tr(t):
            g, tt = t // 4, t % 4
            for w in range(2):
                def atr(tensor, t=t, tt=tt, g=g, w=w):
                    tensor.wait_ge(ve.h, M[f"axn{t}"])
                    if t == 0 and w == 0:
                        tensor.wait_ge(ve.h, M[f"cwt{KD-1}"])
                        tensor.wait_ge(se.h, M["bwt"])
                    if tt == 0 and w == 0 and g >= 1:
                        tensor.wait_ge(pe.h, M[f"axp{g-1}"])
                        tensor.wait_ge(dXN.h, M[f"axsp{g-1}"])
                    if t >= 1:
                        tensor.wait_ge(se.h, M[f"atrv{t-1}_{w}"])
                    for kk in range(4):
                        k = w * 4 + kk
                        pe.inc(tensor.transpose(
                            psS[w][:, kk * 128:(kk + 1) * 128],
                            xtmp[:, k * 128:(k + 1) * 128], ident[:]))
                    M[f"atr{t}_{w}"] = pe.n
                on_t(atr)

        def emit_tileA_trv(t):
            tt = t % 4
            for w in range(2):
                def atrv(scalar, t=t, tt=tt, w=w):
                    scalar.wait_ge(pe.h, M[f"atr{t}_{w}"])
                    dst = movb.ap().rearrange("p (k c) -> p k c", k=KD)[
                        :, w * 4:(w + 1) * 4, tt * 128:(tt + 1) * 128]
                    src = psS[w][:, 0:512].rearrange("p (k c) -> p k c", k=4)
                    swait(scalar); se.inc(scalar.copy(dst, src))
                    M[f"atrv{t}_{w}"] = se.n
                on_s(atrv)

        def emit_axp(g):
            def axp(tensor, g=g):
                tensor.wait_ge(se.h, M[f"atrv{g*4+3}_1"])
                if g >= 2:
                    tensor.wait_ge(ve.h, M[f"axpv{g-2}"])
                else:
                    tensor.wait_ge(ve.h, M["dstash"])
                ps = psS[2 + g % 2]
                for k in range(KD):
                    pe.inc(tensor.matmul(ps[:], bwt[:, k * 128:(k + 1) * 128],
                                         movb[:, k * 512:(k + 1) * 512],
                                         start=(k == 0), stop=(k == KD - 1)))
                M[f"axp{g}"] = pe.n
            on_t(axp)

            def axpv(vector, g=g):
                vector.wait_ge(pe.h, M[f"axp{g}"])
                vwait(vector); ve.inc(vector.tensor_copy(xproj[:, g * 512:(g + 1) * 512],
                                          psS[2 + g % 2][:]))
                M[f"axpv{g}"] = ve.n
            on_v(axpv)

            def axsp(sync, g=g):
                sync.wait_ge(se.h, M[f"atrv{g*4+3}_1"])
                sync.dma_start(out=xnt_dram[g], in_=movb[:, 0:4096]).then_inc(dXN.h, 16)
                dXN.n += 16
                M[f"axsp{g}"] = dXN.n
            on_y(axsp)

        for t in range(max(TT, KH) + 3):
            if 3 <= t <= KH + 2:
                emit_qqu(t - 3)
            if t < TT:
                emit_tileA_ln(t)
            if t < KH:
                emit_qld(t)
            if 1 <= t <= TT:
                emit_tileA_xn(t - 1)
            if 1 <= t <= KH:
                emit_qfold(t - 1)
            if 2 <= t <= TT + 1:
                emit_tileA_trv(t - 2)
                if (t - 2) % 4 == 3:
                    emit_axp((t - 2) // 4)
            if 1 <= t <= TT:
                emit_tileA_tr(t - 1)
            if 1 <= t <= KH:
                emit_qtr1(t - 1)
            if 2 <= t <= KH + 1:
                emit_qred(t - 2)
                emit_qtrs(t - 2)
                emit_qcps(t - 2)
                emit_qbc(t - 2)
        M["phA_ve"] = ve.n

        def cx1(vector):
            vwait(vector); ve.inc(vector.tensor_reduce(scl[:, 3:4], amx[:, 0:TT], AX, MAX))
            M["amx1"] = ve.n
        on_v(cx1)

        def cx2(tensor):
            tensor.wait_ge(ve.h, M["amx1"])
            pe.inc(tensor.transpose(psS[0][0:1, 0:128], scl[:, 3:4], ident[:]))
            M["amx2"] = pe.n
        on_t(cx2)

        def cx3(vector):
            vector.wait_ge(pe.h, M["amx2"])
            vwait(vector); ve.inc(vector.memset(ccs[0:1, 0:8], 0.0))
            vwait(vector); ve.inc(vector.tensor_reduce(ccs[0:1, 0:1], psS[0][0:1, 0:128], AX, MAX))
            M["amx3"] = ve.n
        on_v(cx3)

        def cx4(sync):
            sync.wait_ge(ve.h, M["amx3"])
            sync.dma_start(out=ccx_i.ap(), in_=ccs[0:1, 0:8]).then_inc(dCC.h, 16)
            dCC.n += 16
            M["ccx_up"] = dCC.n
        on_y(cx4)

        def cx5(gpsimd):
            gpsimd.wait_ge(dCC.h, M["ccx_up"])
            ge.inc(gpsimd.collective_compute(
                "AllReduce", MAX, replica_groups=[list(range(N_CORES))],
                ins=[ccx_i.ap().opt()], outs=[ccx_o.ap().opt()]))
            M["cc1"] = ge.n
        on_g(cx5)

        def scan(vector):
            vector.wait_ge(se.h, M["acol"])
            vector.wait_ge(pe.h, M[f"axp{GB-1}"])
            vwait(vector); ve.inc(vector.tensor_copy(bwt[:, 0:D], hs[:, 0:D]))
            M["dbc"] = ve.n
            for b in range(SEQ_PER_CORE):
                vwait(vector); ve.inc(vector.tensor_tensor_scan(
                    hs[:, b * L:(b + 1) * L],
                    acol[:, 0:1].broadcast_to([128, L]),
                    xproj[:, b * L:(b + 1) * L],
                    0.0, MULT, ADD))
            M["scan"] = ve.n
        on_v(scan)


        def cx6(sync):
            sync.wait_ge(ge.h, M["cc1"])
            sync.dma_start(out=ccs[0:1, 8:16], in_=ccx_o.ap()).then_inc(dCC.h, 16)
            dCC.n += 16
            M["ccx_dn"] = dCC.n
        on_y(cx6)

        def cx7(tensor):
            tensor.wait_ge(dCC.h, M["ccx_dn"])
            pe.inc(tensor.matmul(psS[0][:, 0:1], ones[0:1, :], ccs[0:1, 8:9],
                                 start=True, stop=True))
            M["sxb"] = pe.n
        on_t(cx7)

        def cx8(vector):
            vector.wait_ge(pe.h, M["sxb"])
            vwait(vector); ve.inc(vector.reciprocal(scl[:, 0:1], psS[0][:, 0:1]))
            vwait(vector); ve.inc(vector.tensor_scalar(scl[:, 0:1], scl[:, 0:1],
                                        32767.0, None, MULT))
            vwait(vector); ve.inc(vector.tensor_scalar(dv[:, 0:KH], sj[:, 0:KH],
                                        scl[:, 0:1], None, MULT))
            vwait(vector); ve.inc(vector.reciprocal(dv[:, 0:KH], dv[:, 0:KH]))
            M["sxd"] = ve.n
        on_v(cx8)

        def bol(sync):
            sync.wait_ge(ve.h, M["dstash"])
            sync.dma_start(out=rowb[0:1, :], in_=bo_in).then_inc(dW.h, 16)
            dW.n += 16
            M["bol"] = dW.n
        on_y(bol)

        def emit_woq(i):
            def wol(scalar, i=i):
                if i <= 1:
                    scalar.wait_ge(ve.h, M[f"qqu{KH-1}"])
                else:
                    scalar.wait_ge(ve.h, M[f"woq{i-2}"])
                wbuf = qin[:, (i % 2) * D:(i % 2 + 1) * D]
                scalar.dma_start(out=wbuf,
                                 in_=wo_in[i * 128:(i + 1) * 128, :]
                                 ).then_inc(dW.h, 16)
                dW.n += 16
                M[f"wol{i}"] = dW.n
            on_s(wol)

            def woq(vector, i=i):
                vector.wait_ge(dW.h, M[f"wol{i}"])
                wbuf = qin[:, (i % 2) * D:(i % 2 + 1) * D]
                vwait(vector); ve.inc(vector.tensor_reduce(si[:, i:i + 1], wbuf, AX, MAX,
                                            apply_absolute_value=True))
                vwait(vector); ve.inc(vector.reciprocal(scl[:, 4:5], si[:, i:i + 1]))
                vwait(vector); ve.inc(vector.tensor_scalar(si[:, i:i + 1], scl[:, 4:5],
                                            127.0, None, MULT))
                vwait(vector); ve.inc(vector.tensor_scalar(wbuf, wbuf,
                                            si[:, i:i + 1], CM, MULT, ADD))
                vwait(vector); ve.inc(vector.tensor_scalar(woi[:, i * D:(i + 1) * D],
                                            wbuf, CM, None, SUB))
                M[f"woq{i}"] = ve.n
            on_v(woq)

        def emit_pieceB(g, k):
            def bnl(sync, g=g, k=k):
                if g == 0 and k == 0:
                    sync.wait_ge(ve.h, M[f"axn{TT-1}"])
                sync.wait_ge(dXN.h, M[f"axsp{g}"])
                prev = (g, k - 4) if k >= 4 else ((g - 1, KD - 4 + k) if g > 0 else None)
                if prev is not None:
                    sync.wait_ge(ve.h, M[f"blb{prev[0]}_{prev[1]}"])
                s = dXB[k % 4]
                sync.dma_start(out=xbuf[:, (k % 4) * 512:(k % 4 + 1) * 512],
                               in_=xnt_dram[g, :, k * 512:(k + 1) * 512]
                               ).then_inc(s.h, 16)
                s.n += 16
                M[f"bnl{g}_{k}"] = s.n
            on_y(bnl)

            def blb(vector, g=g, k=k):
                vector.wait_ge(dXB[k % 4].h, M[f"bnl{g}_{k}"])
                if k == 0:
                    if g <= 1:
                        vector.wait_ge(ve.h, M["sxd"])
                        vector.wait_ge(ve.h, M["scan"])
                    if g >= 2:
                        vector.wait_ge(pe.h, M[f"bmm{g-2}"])
                mh = movh if g % 2 == 0 else movh2
                xb = xbuf[:, (k % 4) * 512:(k % 4 + 1) * 512]
                xt = xtmp[:, (k % 2) * 512:(k % 2 + 1) * 512]
                hi = mh[:, k * 512:(k + 1) * 512]
                lo = mh[:, 4096 + k * 512:4096 + (k + 1) * 512]
                vwait(vector); ve.inc(vector.tensor_scalar(xt, xb, scl[:, 0:1],
                                            CM, MULT, ADD))
                vwait(vector); ve.inc(vector.tensor_scalar(xt, xt, CM, None, SUB))
                vwait(vector); ve.inc(vector.tensor_copy(hi, xt))
                vwait(vector); ve.inc(vector.tensor_tensor(lo, xt, hi, SUB))
                M[f"blb{g}_{k}"] = ve.n
            on_v(blb)

        def emit_bj(g, j):
            def bmm(tensor, g=g, j=j):
                if j == 0:
                    tensor.wait_ge(ve.h, M[f"blb{g}_{KD-1}"])
                if g == 0:
                    tensor.wait_ge(ve.h, M[f"qqu{j}"])
                prevj = (g, j - 4) if j >= 4 else ((g - 1, KH - 4 + j) if g > 0 else None)
                if prevj is not None:
                    tensor.wait_ge(se.h, M[f"bgl{prevj[0]}_{prevj[1]}"])
                    tensor.wait_ge(ve.h, M[f"brd{prevj[0]}_{prevj[1]}"])
                ps = psS[j % 4]
                mh = movh if g % 2 == 0 else movh2
                for k in range(KD):
                    w = wii[:, k * H + j * 128:k * H + j * 128 + 128]
                    pe.inc(tensor.matmul(ps[:], w, mh[:, k * 512:(k + 1) * 512],
                                         start=(k == 0), stop=False))
                    pe.inc(tensor.matmul(
                        ps[:], w, mh[:, 4096 + k * 512:4096 + (k + 1) * 512],
                        start=False, stop=(k == KD - 1)))
                M[f"bmmj{g}_{j}"] = pe.n
                if j == KH - 1:
                    M[f"bmm{g}"] = pe.n
            on_t(bmm)

            def bgl(scalar, g=g, j=j):
                scalar.wait_ge(pe.h, M[f"bmmj{g}_{j}"])
                if g == 0:
                    scalar.wait_ge(ve.h, M["sxd"])
                    if j < 1:
                        scalar.wait_ge(pe.h, M[f"qtr1_{KH-1}"])
                prevs = (g, j - 4) if j >= 4 else ((g - 1, KH - 4 + j) if g > 0 else None)
                if prevs is not None:
                    scalar.wait_ge(dSP[j % 4].h, M[f"bsp{prevs[0]}_{prevs[1]}"])
                sl = hslab[:, (j % 4) * 512:(j % 4 + 1) * 512]
                swait(scalar); se.inc(scalar.activation(sl, psS[j % 4][:], AF.Gelu,
                                         bias=bint[:, j:j + 1],
                                         scale=dv[:, j:j + 1]))
                M[f"bgl{g}_{j}"] = se.n
            on_s(bgl)

            def bspill(sync, g=g, j=j):
                sync.wait_ge(se.h, M[f"bgl{g}_{j}"])
                sl = hslab[:, (j % 4) * 512:(j % 4 + 1) * 512]
                s = dSP[j % 4]
                sync.dma_start(out=hid_dram[j, g, :, :], in_=sl).then_inc(s.h, 16)
                s.n += 16
                M[f"bsp{g}_{j}"] = s.n
            on_s(bspill)

            def brd(vector, g=g, j=j):
                vector.wait_ge(se.h, M[f"bgl{g}_{j}"])
                sl = hslab[:, (j % 4) * 512:(j % 4 + 1) * 512]
                vwait(vector); ve.inc(vector.tensor_reduce(amhg[:, j:j + 1], sl, AX, MAX,
                                            apply_absolute_value=True))
                M[f"brd{g}_{j}"] = ve.n
            on_v(brd)

        for g in range(GB):
            if g == 0:
                for k in range(KD):
                    emit_pieceB(0, k)
            for j in range(KH):
                emit_bj(g, j)
                if g + 1 < GB and j >= 8 and (j - 8) % 2 == 0 and (j - 8) // 2 < KD:
                    emit_pieceB(g + 1, (j - 8) // 2)
                if GB > 1 and g == 1:
                    emit_woq(j)

            def bam(vector, g=g):
                if g == 0:
                    vwait(vector); ve.inc(vector.tensor_copy(amh[:, 0:KH], amhg[:, 0:KH]))
                else:
                    vwait(vector); ve.inc(vector.tensor_tensor(amh[:, 0:KH], amh[:, 0:KH],
                                                amhg[:, 0:KH], MAX))
                M[f"bam{g}"] = ve.n
            on_v(bam)
            if GB == 1 and g == 0:
                for j in range(KH):
                    emit_woq(j)
        M["woq_all"] = M[f"woq{KH-1}"]

        def ch1(vector):
            vwait(vector); ve.inc(vector.tensor_reduce(scl[:, 5:6], amh[:, 0:KH], AX, MAX))
            M["amh1"] = ve.n
        on_v(ch1)

        def ch2(tensor):
            tensor.wait_ge(ve.h, M["amh1"])
            pe.inc(tensor.transpose(psS[0][0:1, 0:128], scl[:, 5:6], ident[:]))
            M["amh2"] = pe.n
        on_t(ch2)

        def ch3(vector):
            vector.wait_ge(pe.h, M["amh2"])
            vwait(vector); ve.inc(vector.memset(ccs[0:1, 0:8], 0.0))
            vwait(vector); ve.inc(vector.tensor_reduce(ccs[0:1, 0:1], psS[0][0:1, 0:128], AX, MAX))
            M["amh3"] = ve.n
        on_v(ch3)

        def ch4(sync):
            sync.wait_ge(ve.h, M["amh3"])
            sync.dma_start(out=cch_i.ap(), in_=ccs[0:1, 0:8]).then_inc(dCC.h, 16)
            dCC.n += 16
            M["cch_up"] = dCC.n
        on_y(ch4)

        def ch5(gpsimd):
            gpsimd.wait_ge(dCC.h, M["cch_up"])
            ge.inc(gpsimd.collective_compute(
                "AllReduce", MAX, replica_groups=[list(range(N_CORES))],
                ins=[cch_i.ap().opt()], outs=[cch_o.ap().opt()]))
            M["cc2"] = ge.n
        on_g(ch5)

        def ch6(sync):
            sync.wait_ge(ge.h, M["cc2"])
            sync.dma_start(out=ccs[0:1, 8:16], in_=cch_o.ap()).then_inc(dCC.h, 16)
            dCC.n += 16
            M["cch_dn"] = dCC.n
        on_y(ch6)

        def ch7(tensor):
            tensor.wait_ge(dCC.h, M["cch_dn"])
            pe.inc(tensor.matmul(psS[0][:, 0:1], ones[0:1, :], ccs[0:1, 8:9],
                                 start=True, stop=True))
            M["shb"] = pe.n
        on_t(ch7)

        def ch8(vector):
            vector.wait_ge(pe.h, M["shb"])
            vwait(vector); ve.inc(vector.reciprocal(scl[:, 1:2], psS[0][:, 0:1]))
            vwait(vector); ve.inc(vector.tensor_scalar(scl[:, 1:2], scl[:, 1:2],
                                        32767.0, None, MULT))
            vwait(vector); ve.inc(vector.tensor_scalar(civ[:, 0:KH], si[:, 0:KH],
                                        scl[:, 1:2], None, MULT))
            vwait(vector); ve.inc(vector.reciprocal(civ[:, 0:KH], civ[:, 0:KH]))
            M["shd"] = ve.n
        on_v(ch8)

        def dtile(g, sub, nh):
            if g % 2 == 0:
                return psB[sub][:, nh * 512:(nh + 1) * 512]
            return psS[sub * 2 + nh][:, 0:512]

        def dpre(tensor):
            tensor.wait_ge(ve.h, M["scan"])
            tensor.wait_ge(dW.h, M["bol"])
            for sub in range(2):
                stat = hs[:, sub * 128:sub * 128 + 128]
                for nh in range(2):
                    pe.inc(tensor.matmul(
                        psB[sub][:, nh * 512:(nh + 1) * 512],
                        stat, cwt[:, nh * 512:(nh + 1) * 512],
                        start=True, stop=False))
                for nh in range(2):
                    pe.inc(tensor.matmul(
                        psB[sub][:, nh * 512:(nh + 1) * 512],
                        ones[0:1, :], rowb[0:1, nh * 512:(nh + 1) * 512],
                        start=False, stop=False))
        on_t(dpre)

        deferred = []
        for g in range(GD):
            tok0 = g * 256
            gB, hB = tok0 // 512, (tok0 % 512) // 256

            def dxl(sync, g=g, tok0=tok0):
                if g == 0:
                    sync.wait_ge(ve.h, M[f"blb{GB-1}_{KD-1}"])
                else:
                    sync.wait_ge(ve.h, M[f"dev{g-1}"])
                sync.dma_start(
                    out=xbuf[:, 0:2 * D].rearrange("p (s d) -> p s d", s=2),
                    in_=x_in[tok0:tok0 + 256, :].rearrange("(s p) d -> p s d", s=2),
                ).then_inc(dXD.h, 16)
                dXD.n += 16
                M[f"dxl{g}"] = dXD.n

            if g == 0:
                on_y(dxl)

            pend_mm = []
            for i in range(KH):
                sl = i % 2
                if i == 2:
                    if deferred:
                        deferred[0]()
                    for fn in pend_mm:
                        on_t(fn)
                    pend_mm = []
                if i == 3 and deferred:
                    deferred[1]()
                if i == min(5, KH - 3) and deferred:
                    deferred[2]()
                if i == min(7, KH - 2) and deferred:
                    deferred[3]()
                    deferred = []
                    on_y(dxl)

                def dsr(sync, g=g, gB=gB, hB=hB, i=i):
                    if g == 0 and i < 8:
                        sync.wait_ge(ve.h, M["woq_all"])
                    sync.wait_ge(dSP[i % 4].h, M[f"bsp{gB}_{i}"])
                    prev = (g, i - 8) if i >= 8 else ((g - 1, KH - 8 + i) if g > 0 else None)
                    if prev is not None:
                        sync.wait_ge(ve.h, M[f"dp{prev[0]}_{prev[1]}"])
                    s = dSR[i % 8]
                    sync.dma_start(
                        out=qin[:, (i % 8) * 256:(i % 8) * 256 + 256],
                        in_=hid_dram[i, gB, :, hB * 256:(hB + 1) * 256],
                    ).then_inc(s.h, 16)
                    s.n += 16
                    M[f"dsr{g}_{i}"] = s.n
                on_s(dsr)

                def dp_pair(vector, g=g, i0=i - 1):
                    q = (i0 // 2) % 2
                    vector.wait_ge(dSR[i0 % 8].h, M[f"dsr{g}_{i0}"])
                    vector.wait_ge(dSR[(i0 + 1) % 8].h, M[f"dsr{g}_{i0+1}"])
                    if g == 0 and i0 == 0:
                        vector.wait_ge(ve.h, M["shd"])
                        for s4 in range(4):
                            vector.wait_ge(dSP[s4].h, dSP[s4].n)
                    prev = ((g, i0 - 3) if i0 >= 4
                            else ((g - 1, KH - 3 + i0) if g > 0 else None))
                    if prev is not None:
                        vector.wait_ge(pe.h, M[f"dmm{prev[0]}_{prev[1]}"])
                    buf2 = qin[:, (i0 % 8) * 256:(i0 % 8) * 256 + 512]
                    p = pF(q)
                    vwait(vector); ve.inc(vector.tensor_scalar(p, buf2, scl[:, 1:2],
                                                CM, MULT, ADD))
                    M[f"dp{g}_{i0}"] = ve.n
                    M[f"dp{g}_{i0+1}"] = ve.n
                    vwait(vector); ve.inc(vector.tensor_scalar(p[:, 0:256], p[:, 0:256],
                                                CM, civ[:, i0:i0 + 1], SUB, MULT))
                    vwait(vector); ve.inc(vector.tensor_scalar(p[:, 256:512], p[:, 256:512],
                                                CM, civ[:, i0 + 1:i0 + 2], SUB, MULT))
                    vwait(vector); ve.inc(vector.tensor_copy(plbp(q, 0), p))
                    vwait(vector); ve.inc(vector.tensor_tensor(plbp(q, 1), p, plbp(q, 0), SUB))
                    M[f"dpl{g}_{i0}"] = ve.n
                    M[f"dpl{g}_{i0+1}"] = ve.n
                if i % 2 == 1:
                    on_v(dp_pair)

                def mk_dmm(ii, g=g, tok0=tok0):
                  def dmm(tensor, g=g, i=ii, tok0=tok0):
                    tensor.wait_ge(ve.h, M[f"dpl{g}_{i}"])
                    if i == 0 and g >= 1:
                        if g >= 2:
                            tensor.wait_ge(ve.h, M[f"devc{g-2}"])
                        else:
                            tensor.wait_ge(se.h, M[f"bgl{GB-1}_{KH-1}"])
                        for sub in range(2):
                            stat = hs[:, tok0 + sub * 128:tok0 + sub * 128 + 128]
                            for nh in range(2):
                                pe.inc(tensor.matmul(
                                    dtile(g, sub, nh),
                                    stat, cwt[:, nh * 512:(nh + 1) * 512],
                                    start=(nh >= 0), stop=False))
                            for nh in range(2):
                                pe.inc(tensor.matmul(
                                    dtile(g, sub, nh),
                                    ones[0:1, :], rowb[0:1, nh * 512:(nh + 1) * 512],
                                    start=False, stop=False))
                    q, io = (i // 2) % 2, i % 2
                    for sub in range(2):
                        for lb in range(2):
                            stat = plbp(q, lb)[:, io * 256 + sub * 128:
                                               io * 256 + sub * 128 + 128]
                            for nh in range(2):
                                pe.inc(tensor.matmul(
                                    dtile(g, sub, nh),
                                    stat,
                                    woi[:, i * D + nh * 512:i * D + (nh + 1) * 512],
                                    start=False, stop=(i == KH - 1 and lb == 1)))
                    M[f"dmm{g}_{i}"] = pe.n
                  return dmm
                if i % 2 == 1:
                    for ii in (i - 1, i):
                        if ii < 2:
                            pend_mm.append(mk_dmm(ii))
                        else:
                            on_t(mk_dmm(ii))
            M[f"dy{g}"] = M[f"dmm{g}_{KH-1}"]

            def mk_tail(g=g, tok0=tok0):
                def emit_devc():
                    def devc(vector, g=g):
                        vector.wait_ge(pe.h, M[f"dy{g}"])
                        if g == 0:
                            vector.wait_ge(pe.h, M[f"bmm{GB-1}"])
                        if g >= 1:
                            vector.wait_ge(dOUT.h, M[f"dout{g-1}"])
                        for sub in range(2):
                            ot = outst[:, sub * D:(sub + 1) * D]
                            if g % 2 == 0:
                                vwait(vector); ve.inc(vector.tensor_copy(ot, psB[sub][:]))
                            else:
                                for nh in range(2):
                                    vwait(vector); ve.inc(vector.tensor_copy(
                                        ot[:, nh * 512:(nh + 1) * 512],
                                        dtile(g, sub, nh)))
                        M[f"devc{g}"] = ve.n
                    on_v(devc)

                def mk_dev(sub):
                    def emit_dev():
                        def dev(vector, g=g, sub=sub):
                            if sub == 0:
                                vector.wait_ge(dXD.h, M[f"dxl{g}"])
                            t = g * 2 + sub
                            xt = xbuf[:, sub * D:(sub + 1) * D]
                            ot = outst[:, sub * D:(sub + 1) * D]
                            vwait(vector); ve.inc(vector.tensor_scalar(
                                xtmp[:, 0:D], xt, mu[:, t:t + 1],
                                rsig[:, t:t + 1], SUB, MULT))
                            vwait(vector); ve.inc(vector.tensor_tensor(
                                xtmp[:, 0:D], xtmp[:, 0:D], bwt[:, 0:D], MULT))
                            vwait(vector); ve.inc(vector.tensor_tensor(
                                xtmp[:, 0:D], xtmp[:, 0:D], xt, ADD))
                            vwait(vector); ve.inc(vector.tensor_tensor(
                                ot, ot, xtmp[:, 0:D], ADD))
                            if sub == 1:
                                M[f"dev{g}"] = ve.n
                        on_v(dev)
                    return emit_dev

                def emit_dout():
                    def dout(sync, g=g, tok0=tok0):
                        sync.wait_ge(ve.h, M[f"dev{g}"])
                        sync.dma_start(
                            out=y_out[tok0:tok0 + 256, :].rearrange("(s p) d -> p s d", s=2),
                            in_=outst[:, 0:2 * D].rearrange("p (s d) -> p s d", s=2),
                        ).then_inc(dOUT.h, 16)
                        dOUT.n += 16
                        M[f"dout{g}"] = dOUT.n
                    on_y(dout)
                return [emit_devc, mk_dev(0), mk_dev(1), emit_dout]
            deferred = mk_tail()

        for fn in deferred:
            fn()
        deferred = []

        if debug:
            def dbg(sync):
                sync.wait_ge(ve.h, ve.n)
                sync.wait_ge(se.h, se.n)
                sync.wait_ge(pe.h, pe.n)
                for src_ap, c0 in [(mu, 0), (rsig, TT), (rss, 2 * TT), (amx, 3 * TT),
                                   (sj, 4 * TT), (si, 4 * TT + KH), (dv, 4 * TT + 2 * KH),
                                   (civ, 4 * TT + 3 * KH)]:
                    w = src_ap.ap().shape[1]
                    sync.dma_start(out=dbg_small[:, c0:c0 + w], in_=src_ap[:, 0:w]
                                   ).then_inc(dOUT.h, 16)
                    dOUT.n += 16
                sync.dma_start(out=dbg_small[:, 248:252].rearrange("p (a b) -> p a b", a=4),
                               in_=scl[:, 0:4].rearrange("p (a b) -> p a b", a=4),
                               allow_slow_non_contiguous=True).then_inc(dOUT.h, 16)
                sync.dma_start(out=dbg_small[:, 252:253], in_=acol[:, 0:1],
                               allow_slow_non_contiguous=True).then_inc(dOUT.h, 16)
                dOUT.n += 32
                sync.dma_start(out=dbg_xproj, in_=xproj[:, 0:T]).then_inc(dOUT.h, 16)
                sync.dma_start(out=dbg_hs, in_=hs[:, 0:T]).then_inc(dOUT.h, 16)
                sync.dma_start(out=dbg_wii, in_=wii[:]).then_inc(dOUT.h, 16)
                sync.dma_start(out=dbg_woi, in_=woi[:]).then_inc(dOUT.h, 16)
                sync.dma_start(out=dbg_movb, in_=movb[:]).then_inc(dOUT.h, 16)
                dOUT.n += 80
            on_y(dbg)

        def fin(sync):
            sync.wait_ge(dOUT.h, dOUT.n)
        on_y(fin)

    return nc


def kernel(L0_attn_x, L0_attn_B_w, L0_attn_C_w, L0_attn_log_A, L0_attn_D_param,
           L0_ffn_W_in, L0_ffn_b_in, L0_ffn_W_out, L0_ffn_b_out):
    cfg = Cfg()
    B, L, D = L0_attn_x.shape
    assert (B, L, D) == (16, 2048, 1024)
    x = np.ascontiguousarray(np.asarray(L0_attn_x, dtype=np.float32))
    shards = [np.ascontiguousarray(
        x[c * SEQ_PER_CORE:(c + 1) * SEQ_PER_CORE].reshape(cfg.T, D))
        for c in range(N_CORES)]
    common = {
        "bw": np.ascontiguousarray(np.asarray(L0_attn_B_w, np.float32)),
        "cw": np.ascontiguousarray(np.asarray(L0_attn_C_w, np.float32)),
        "la": np.ascontiguousarray(np.asarray(L0_attn_log_A, np.float32).reshape(1, 128)),
        "dp": np.ascontiguousarray(np.asarray(L0_attn_D_param, np.float32).reshape(1, D)),
        "wi": np.ascontiguousarray(np.asarray(L0_ffn_W_in, np.float32)),
        "bi": np.ascontiguousarray(np.asarray(L0_ffn_b_in, np.float32).reshape(cfg.KH, 128)),
        "wo": np.ascontiguousarray(np.asarray(L0_ffn_W_out, np.float32)),
        "bo": np.ascontiguousarray(np.asarray(L0_ffn_b_out, np.float32).reshape(1, D)),
        "ident": np.eye(128, dtype=np.float32),
    }
    nc = build(cfg)
    in_maps = [dict(common, x=s) for s in shards]
    res = run_bass_kernel_spmd(nc, in_maps, core_ids=list(range(N_CORES)))
    out = np.concatenate([np.asarray(r["y"]).reshape(SEQ_PER_CORE, L, D)
                          for r in res.results], axis=0)
    return out


# version 67 (final)
# speedup vs baseline: 1.0366x; 1.0133x over previous
import numpy as np
from contextlib import ExitStack

import concourse.bass as bass
import concourse.mybir as mybir
from concourse.alu_op_type import AluOpType
from concourse.bass_utils import run_bass_kernel_spmd

F32 = mybir.dt.float32
BF16 = mybir.dt.bfloat16
F16 = mybir.dt.float16
AF = mybir.ActivationFunctionType
AX = mybir.AxisListType.X
MULT = AluOpType.mult
ADD = AluOpType.add
SUB = AluOpType.subtract
MAX = AluOpType.max
MAXOP = AluOpType.max
MINOP = AluOpType.min

CM = float(3 << 22)
EPS = 1e-5
N_CORES = 8
SEQ_PER_CORE = 2


class Cfg:
    def __init__(self, L=2048, D=1024, H=4096):
        self.L, self.D, self.H = L, D, H
        self.T = SEQ_PER_CORE * L
        self.KD = D // 128
        self.KH = H // 128
        self.TT = self.T // 128
        self.GB = self.T // 512
        self.GD = self.T // 256


class Sem:
    def __init__(self, st, nc, name):
        self.h = st.enter_context(nc.semaphore(name))
        self.n = 0

    def inc(self, instr, k=1):
        instr.then_inc(self.h, k)
        self.n += k
        return self.n


def build(cfg: Cfg, debug=False):
    nc = bass.Bass("TRN2", target_bir_lowering=False,
                   dynamic_dma_scratch_size=2560,
                   detect_race_conditions=False)
    L, D, H, T = cfg.L, cfg.D, cfg.H, cfg.T
    KD, KH, TT, GB, GD = cfg.KD, cfg.KH, cfg.TT, cfg.GB, cfg.GD

    x_in = nc.dram_tensor("x", [T, D], F32, kind="ExternalInput").ap()
    bw_in = nc.dram_tensor("bw", [128, D], F32, kind="ExternalInput").ap()
    cw_in = nc.dram_tensor("cw", [D, 128], F32, kind="ExternalInput").ap()
    la_in = nc.dram_tensor("la", [1, 128], F32, kind="ExternalInput").ap()
    dp_in = nc.dram_tensor("dp", [1, D], F32, kind="ExternalInput").ap()
    wi_in = nc.dram_tensor("wi", [D, H], F32, kind="ExternalInput").ap()
    bi_in = nc.dram_tensor("bi", [KH, 128], F32, kind="ExternalInput").ap()
    wo_in = nc.dram_tensor("wo", [H, D], F32, kind="ExternalInput").ap()
    bo_in = nc.dram_tensor("bo", [1, D], F32, kind="ExternalInput").ap()
    id_in = nc.dram_tensor("ident", [128, 128], F32, kind="ExternalInput").ap()
    y_out = nc.dram_tensor("y", [T, D], F32, kind="ExternalOutput").ap()

    xnt_dram = nc.dram_tensor("xnt_spill", [GB, 128, 4096], F32).ap()
    hid_dram = nc.dram_tensor("hid_spill", [KH, GB, 128, 512], F32,
                              kind="ExternalOutput" if debug else "Internal").ap()
    if debug:
        dbg_small = nc.dram_tensor("dbg_small", [128, 256], F32, kind="ExternalOutput").ap()
        dbg_xproj = nc.dram_tensor("dbg_xproj", [128, T], F32, kind="ExternalOutput").ap()
        dbg_hs = nc.dram_tensor("dbg_hs", [128, T], F32, kind="ExternalOutput").ap()
        dbg_wii = nc.dram_tensor("dbg_wii", [128, KD * H], BF16, kind="ExternalOutput").ap()
        dbg_woi = nc.dram_tensor("dbg_woi", [128, KH * D], BF16, kind="ExternalOutput").ap()
        dbg_movb = nc.dram_tensor("dbg_movb", [128, 4096], F32, kind="ExternalOutput").ap()
    ccx_i = nc.dram_tensor("ccx_i", [1, 8], F32)
    ccx_o = nc.dram_tensor("ccx_o", [1, 8], F32)
    cch_i = nc.dram_tensor("cch_i", [1, 8], F32)
    cch_o = nc.dram_tensor("cch_o", [1, 8], F32)

    with ExitStack() as st:
        E = st.enter_context

        def sb(name, p, f, dt=F32):
            return E(nc.sbuf_tensor(name, [p, f], dt))

        ident = sb("identt", 128, 128)
        ones = sb("ones", 1, 128)
        srow = sb("srow", 1, 256)
        arow = sb("arow", 1, 128)
        acol = sb("acol", 128, 1)
        epsc = sb("epsc", 128, 1)
        cmc = sb("cmc", 128, 1)
        cmn = sb("cmn", 128, 1)
        identb = sb("identb", 128, 128, BF16)
        cwt = sb("cwt", 128, D)
        bwt = sb("bwt", 128, D)
        bint = sb("bint", 128, KH)
        binrow = sb("binrow", KH, 128)
        rowb = sb("rowb", 1, D)
        mu = sb("mu", 128, TT)
        rsig = sb("rsig", 128, TT)
        rss = sb("rss", 128, TT)
        sj = sb("sj", 128, KH)
        si = sb("si", 128, KH)
        dv = sb("dv", 128, KH)
        civ = sb("civ", 128, KH)
        amx = sb("amx", 128, TT)
        amh = sb("amh", 128, KH)
        amhg = sb("amhg", 128, KH)
        scl = sb("scl", 128, 8)
        ccs = sb("ccs", 1, 16)
        bnst = sb("bnst", 128, 12)
        bnag = sb("bnag", 128, 2)
        xproj = sb("xproj", 128, max(T, 2 * D, 4096))
        hs = sb("hs", 128, max(T, D))
        wii = sb("wii", 128, KD * H, BF16)
        woi = sb("woi", 128, KH * D, BF16)
        movb = sb("movb", 128, 4096)
        xbuf = sb("xbuf", 128, 2 * D)
        xtmp = sb("xtmp", 128, D)
        qin = sb("qin", 128, 2 * D)
        hslab = sb("hslab", 128, 4 * 512)

        movh = movb.ap().bitcast(BF16)
        movh2 = xproj[:, 0:4096].bitcast(BF16)
        qin_lo = qin[:, 0:D]
        qt = qin[:, D:2 * D]
        outst = xproj[:, 0:2 * D]
        plF = hslab[:, 0:512]
        plR = hslab[:, 512:1024]
        plB = hslab[:, 1024:2048].bitcast(F16)

        def pF(q):
            return plF[:, 0:512] if q == 0 else plR[:, 0:512]

        def plbp(q, which):
            return plB[:, q * 1024 + which * 512:q * 1024 + (which + 1) * 512]

        psS = [E(nc.psum_tensor(f"psS{i}", [128, 512], F32)) for i in range(4)]
        psB = [E(nc.psum_tensor(f"psB{i}", [128, 1024], F32)) for i in range(2)]

        dX = [Sem(st, nc, f"dX{i}") for i in range(2)]
        dXB = [Sem(st, nc, f"dXB{i}") for i in range(4)]
        dXN = Sem(st, nc, "dXN")
        dXD = Sem(st, nc, "dXD")
        dW = Sem(st, nc, "dW")
        dSP = [Sem(st, nc, f"dSP{i}") for i in range(4)]
        dSR = [Sem(st, nc, f"dSR{i}") for i in range(8)]
        dOUT = Sem(st, nc, "dOUT")
        dCC = Sem(st, nc, "dCC")
        pe = Sem(st, nc, "pe")
        ve = Sem(st, nc, "ve")
        se = Sem(st, nc, "se")
        ge = Sem(st, nc, "ge")

        block = E(nc.Block())
        M = {}
        on_t, on_v, on_s = block.tensor, block.vector, block.scalar
        on_y, on_g = block.sync, block.gpsimd

        def vwait(v):
            v.wait_ge(ve.h, ve.n)

        def swait(s):
            s.wait_ge(se.h, se.n)

        def ld0(sync):
            sync.dma_start(out=ident[:], in_=id_in).then_inc(dW.h, 16)
            sync.dma_start(out=arow[0:1, :], in_=la_in).then_inc(dW.h, 16)
            sync.dma_start(out=binrow[0:KH, :], in_=bi_in).then_inc(dW.h, 16)
            sync.dma_start(out=rowb[0:1, :], in_=dp_in).then_inc(dW.h, 16)
            for k in range(KD):
                sync.dma_start(out=qin[:, k * 128:(k + 1) * 128],
                               in_=cw_in[k * 128:(k + 1) * 128, :]).then_inc(dW.h, 16)
            sync.dma_start(out=xtmp[:, 0:D], in_=bw_in).then_inc(dW.h, 16)
            dW.n += 16 * (5 + KD)
            M["ld0"] = dW.n
        on_y(ld0)

        def c0v(vector):
            vector.wait_ge(dW.h, M["ld0"])
            vwait(vector); ve.inc(vector.memset(ones[:], 1.0))
            vwait(vector); ve.inc(vector.memset(epsc[:], EPS))
            vwait(vector); ve.inc(vector.memset(cmc[:], CM))
            vwait(vector); ve.inc(vector.memset(cmn[:], -CM))
            vwait(vector); ve.inc(vector.tensor_copy(identb[:], ident[:]))
            M["ones"] = ve.n
        on_v(c0v)

        def c0t(tensor):
            tensor.wait_ge(dW.h, M["ld0"])
            tensor.wait_ge(ve.h, M["ones"])
            pe.inc(tensor.transpose(psS[0][:, 0:1], arow[0:1, :], ident[0:1, 0:1]))
            pe.inc(tensor.transpose(psS[1][:, 0:KH], binrow[0:KH, :], ident[0:KH, 0:KH]))
            M["tr_la"] = pe.n
            for h in range(2):
                pe.inc(tensor.matmul(psS[2 + h][:, 0:512], ones[0:1, :],
                                     rowb[0:1, h * 512:(h + 1) * 512],
                                     start=True, stop=True))
            M["dpb"] = pe.n
        on_t(c0t)

        def c0v2(vector):
            vector.wait_ge(pe.h, M["tr_la"])
            vwait(vector); ve.inc(vector.tensor_copy(acol[:, 0:1], psS[0][:, 0:1]))
            vwait(vector); ve.inc(vector.tensor_copy(bint[:, 0:KH], psS[1][:, 0:KH]))
            M["bint"] = ve.n
            vector.wait_ge(pe.h, M["dpb"])
            for h in range(2):
                vwait(vector); ve.inc(vector.tensor_copy(hs[:, h * 512:(h + 1) * 512],
                                          psS[2 + h][:, 0:512]))
            M["dstash"] = ve.n
        on_v(c0v2)

        def c0s(scalar):
            scalar.wait_ge(ve.h, M["bint"])
            swait(scalar); se.inc(scalar.activation(acol[:, 0:1], acol[:, 0:1], AF.Exp))
            M["acol"] = se.n
        on_s(c0s)

        for k in range(KD):
            def cwk(tensor, k=k):
                tensor.wait_ge(ve.h, M["bint"] if k == 0 else M[f"cwt{k-1}"])
                pe.inc(tensor.transpose(psS[0][:, 0:128],
                                        qin[:, k * 128:(k + 1) * 128], ident[:]))
                M[f"cwtr{k}"] = pe.n
            on_t(cwk)

            def cwkv(vector, k=k):
                vector.wait_ge(pe.h, M[f"cwtr{k}"])
                vwait(vector); ve.inc(vector.tensor_copy(cwt[:, k * 128:(k + 1) * 128],
                                          psS[0][:, 0:128]))
                M[f"cwt{k}"] = ve.n
            on_v(cwkv)

        for k in range(KD):
            def bwk(tensor, k=k):
                if k == 0:
                    tensor.wait_ge(ve.h, M["bint"])
                else:
                    tensor.wait_ge(se.h, M[f"bwt{k-1}"])
                pe.inc(tensor.transpose(psS[1][:, 0:128],
                                        xtmp[:, k * 128:(k + 1) * 128], ident[:]))
                M[f"bwtr{k}"] = pe.n
            on_t(bwk)

            def bwkv(scalar, k=k):
                scalar.wait_ge(pe.h, M[f"bwtr{k}"])
                swait(scalar); se.inc(scalar.copy(bwt[:, k * 128:(k + 1) * 128], psS[1][:, 0:128]))
                M[f"bwt{k}"] = se.n
            on_s(bwkv)
        M["bwt"] = se.n
        M["bwtr_last"] = None

        QF = hslab[:, 0:128]
        qslot3 = woi.ap().bitcast(F32)[:, 0:D]

        def qbuf(j):
            s = j % 3
            return qin[:, s * D:(s + 1) * D] if s < 2 else qslot3

        def emit_qld(j):
            def ql(scalar, j=j):
                if j <= 1:
                    scalar.wait_ge(pe.h, M[f"cwtr{KD-1}"])
                elif j >= 3:
                    scalar.wait_ge(ve.h, M[f"qqu{j-3}"])
                buf = qbuf(j)
                scalar.dma_start(
                    out=buf.rearrange("p (k c) -> p k c", k=KD),
                    in_=wi_in[:, j * 128:(j + 1) * 128].rearrange(
                        "(k p) c -> p k c", k=KD),
                ).then_inc(dW.h, 16)
                dW.n += 16
                M[f"qld{j}"] = dW.n
            on_s(ql)

        def emit_qfold(j):
            def qfold(vector, j=j):
                vector.wait_ge(dW.h, M[f"qld{j}"])
                if j >= 1:
                    vector.wait_ge(pe.h, M[f"qtr1_{j-1}"])
                buf = qbuf(j)
                in3 = buf.rearrange("p (k c) -> p c k", k=KD)
                vwait(vector); ve.inc(vector.tensor_reduce(QF[:, 0:128], in3, AX, MAX,
                                            apply_absolute_value=True))
                M[f"qfold{j}"] = ve.n
            on_v(qfold)

        def emit_qtr1(j):
            def qtr1(tensor, j=j):
                tensor.wait_ge(ve.h, M[f"qfold{j}"])
                if j == 0:
                    tensor.wait_ge(se.h, M["bwt"])
                pe.inc(tensor.transpose(psB[0][:, (j % 2) * 128:(j % 2 + 1) * 128],
                                        QF[:, 0:128], ident[:]))
                M[f"qtr1_{j}"] = pe.n
            on_t(qtr1)

        def emit_qred(j):
            def qred(vector, j=j):
                vector.wait_ge(pe.h, M[f"qtr1_{j}"])
                sl = psB[0][:, (j % 2) * 128:(j % 2 + 1) * 128]
                vwait(vector); ve.inc(vector.tensor_reduce(sj[:, j:j + 1], sl, AX, MAX))
                vwait(vector); ve.inc(vector.reciprocal(scl[:, 2:3], sj[:, j:j + 1]))
                vwait(vector); ve.inc(vector.tensor_scalar(sj[:, j:j + 1], scl[:, 2:3],
                                            127.0, None, MULT))
                M[f"qred{j}"] = ve.n
            on_v(qred)

        def emit_qtrs(j):
            def qtrs(tensor, j=j):
                tensor.wait_ge(ve.h, M[f"qred{j}"])
                pe.inc(tensor.transpose(
                    psB[0][0:1, 256 + (j % 2) * 128:384 + (j % 2) * 128],
                    sj[:, j:j + 1], ident[:]))
                M[f"qtrs{j}"] = pe.n
            on_t(qtrs)

        def emit_qcps(j):
            def qcps(vector, j=j):
                vector.wait_ge(pe.h, M[f"qtrs{j}"])
                vwait(vector); ve.inc(vector.tensor_copy(
                    srow[0:1, (j % 2) * 128:(j % 2 + 1) * 128],
                    psB[0][0:1, 256 + (j % 2) * 128:384 + (j % 2) * 128]))
                M[f"qcps{j}"] = ve.n
            on_v(qcps)

        def emit_qbc(j):
            def qbc(tensor, j=j):
                tensor.wait_ge(ve.h, M[f"qcps{j}"])
                pe.inc(tensor.matmul(
                    psB[0][:, 512 + (j % 2) * 128:640 + (j % 2) * 128],
                    ones[0:1, :], srow[0:1, (j % 2) * 128:(j % 2 + 1) * 128],
                    start=True, stop=True))
                M[f"qbc{j}"] = pe.n
            on_t(qbc)

        def emit_qqu(j):
            def qqu(vector, j=j):
                vector.wait_ge(pe.h, M[f"qbc{j}"])
                buf = qbuf(j)
                b3 = buf.rearrange("p (k c) -> p k c", k=KD)
                sbc3 = psB[0][:, 512 + (j % 2) * 128:640 + (j % 2) * 128].rearrange(
                    "p (k c) -> p k c", k=1).broadcast_to([128, KD, 128])
                dst = wii.ap().rearrange("p (k h) -> p k h", k=KD)[
                    :, :, j * 128:(j + 1) * 128]
                vwait(vector); ve.inc(vector.tensor_tensor(b3, b3, sbc3, MULT))
                vwait(vector); ve.inc(vector.tensor_scalar(dst, b3, CM, CM, ADD, SUB))
                M[f"qqu{j}"] = ve.n
            on_v(qqu)


        def emit_tileA_ln(t):
            def axl(sync, t=t):
                if t >= 2:
                    sync.wait_ge(ve.h, M[f"axn{t-2}"])
                s = dX[t % 2]
                sync.dma_start(out=xbuf[:, (t % 2) * D:(t % 2 + 1) * D],
                               in_=x_in[t * 128:(t + 1) * 128, :]).then_inc(s.h, 16)
                s.n += 16
                M[f"axl{t}"] = s.n
            on_y(axl)

            def aln(vector, t=t):
                vector.wait_ge(dX[t % 2].h, M[f"axl{t}"])
                xt = xbuf[:, (t % 2) * D:(t % 2 + 1) * D]
                for cdx in range(D // 512):
                    vwait(vector); ve.inc(vector.bn_stats(bnst[:, 6 * cdx:6 * cdx + 6],
                                           xt[:, cdx * 512:(cdx + 1) * 512]))
                vwait(vector); ve.inc(vector.bn_aggr(bnag[:], bnst[:, 0:6 * (D // 512)]))
                vwait(vector); ve.inc(vector.tensor_copy(mu[:, t:t + 1], bnag[:, 0:1]))
                vwait(vector); ve.inc(vector.tensor_copy(rsig[:, t:t + 1], bnag[:, 1:2]))
                M[f"avar{t}"] = ve.n
            on_v(aln)

            def asd(scalar, t=t):
                scalar.wait_ge(ve.h, M[f"avar{t}"])
                swait(scalar); se.inc(scalar.activation(rsig[:, t:t + 1], rsig[:, t:t + 1],
                                         AF.Sqrt, bias=epsc[:, 0:1]))
                M[f"asd{t}"] = se.n
            on_s(asd)

        def emit_tileA_xn(t):
            def axn(vector, t=t):
                vector.wait_ge(se.h, M[f"asd{t}"])
                if t == 0:
                    vector.wait_ge(pe.h, M[f"bwtr{KD-1}"])
                else:
                    vector.wait_ge(pe.h, M[f"atr{t-1}_1"])
                xt = xbuf[:, (t % 2) * D:(t % 2 + 1) * D]
                vwait(vector); ve.inc(vector.reciprocal(rsig[:, t:t + 1], rsig[:, t:t + 1]))
                vwait(vector); ve.inc(vector.tensor_scalar(xtmp[:, 0:D], xt, mu[:, t:t + 1],
                                            rsig[:, t:t + 1], SUB, MULT))
                vwait(vector); ve.inc(vector.tensor_reduce(amx[:, t:t + 1], xtmp[:, 0:D],
                                            AX, MAX, apply_absolute_value=True))
                M[f"axn{t}"] = ve.n
            on_v(axn)

        def emit_tileA_tr(t):
            g, tt = t // 4, t % 4
            for w in range(2):
                def atr(tensor, t=t, tt=tt, g=g, w=w):
                    tensor.wait_ge(ve.h, M[f"axn{t}"])
                    if t == 0 and w == 0:
                        tensor.wait_ge(ve.h, M[f"cwt{KD-1}"])
                        tensor.wait_ge(se.h, M["bwt"])
                    if tt == 0 and w == 0 and g >= 1:
                        tensor.wait_ge(pe.h, M[f"axp{g-1}"])
                        tensor.wait_ge(dXN.h, M[f"axsp{g-1}"])
                    if t >= 1:
                        tensor.wait_ge(se.h, M[f"atrv{t-1}_{w}"])
                    for kk in range(4):
                        k = w * 4 + kk
                        pe.inc(tensor.transpose(
                            psS[w][:, kk * 128:(kk + 1) * 128],
                            xtmp[:, k * 128:(k + 1) * 128], ident[:]))
                    M[f"atr{t}_{w}"] = pe.n
                on_t(atr)

        def emit_tileA_trv(t):
            tt = t % 4
            for w in range(2):
                def atrv(scalar, t=t, tt=tt, w=w):
                    scalar.wait_ge(pe.h, M[f"atr{t}_{w}"])
                    dst = movb.ap().rearrange("p (k c) -> p k c", k=KD)[
                        :, w * 4:(w + 1) * 4, tt * 128:(tt + 1) * 128]
                    src = psS[w][:, 0:512].rearrange("p (k c) -> p k c", k=4)
                    swait(scalar); se.inc(scalar.copy(dst, src))
                    M[f"atrv{t}_{w}"] = se.n
                on_s(atrv)

        def emit_axp(g):
            def axp(tensor, g=g):
                tensor.wait_ge(se.h, M[f"atrv{g*4+3}_1"])
                if g >= 2:
                    tensor.wait_ge(ve.h, M[f"axpv{g-2}"])
                else:
                    tensor.wait_ge(ve.h, M["dstash"])
                ps = psS[2 + g % 2]
                for k in range(KD):
                    pe.inc(tensor.matmul(ps[:], bwt[:, k * 128:(k + 1) * 128],
                                         movb[:, k * 512:(k + 1) * 512],
                                         start=(k == 0), stop=(k == KD - 1)))
                M[f"axp{g}"] = pe.n
            on_t(axp)

            def axpv(vector, g=g):
                vector.wait_ge(pe.h, M[f"axp{g}"])
                vwait(vector); ve.inc(vector.tensor_copy(xproj[:, g * 512:(g + 1) * 512],
                                          psS[2 + g % 2][:]))
                M[f"axpv{g}"] = ve.n
            on_v(axpv)

            def axsp(sync, g=g):
                sync.wait_ge(se.h, M[f"atrv{g*4+3}_1"])
                sync.dma_start(out=xnt_dram[g], in_=movb[:, 0:4096]).then_inc(dXN.h, 16)
                dXN.n += 16
                M[f"axsp{g}"] = dXN.n
            on_y(axsp)

        for t in range(max(TT, KH) + 3):
            if 3 <= t <= KH + 2:
                emit_qqu(t - 3)
            if t < TT:
                emit_tileA_ln(t)
            if t < KH:
                emit_qld(t)
            if 1 <= t <= TT:
                emit_tileA_xn(t - 1)
            if 1 <= t <= KH:
                emit_qfold(t - 1)
            if 2 <= t <= TT + 1:
                emit_tileA_trv(t - 2)
                if (t - 2) % 4 == 3:
                    emit_axp((t - 2) // 4)
            if 1 <= t <= TT:
                emit_tileA_tr(t - 1)
            if 1 <= t <= KH:
                emit_qtr1(t - 1)
            if 2 <= t <= KH + 1:
                emit_qred(t - 2)
                emit_qtrs(t - 2)
                emit_qcps(t - 2)
                emit_qbc(t - 2)
        M["phA_ve"] = ve.n

        def cx1(vector):
            vwait(vector); ve.inc(vector.tensor_reduce(scl[:, 3:4], amx[:, 0:TT], AX, MAX))
            M["amx1"] = ve.n
        on_v(cx1)

        def cx2(tensor):
            tensor.wait_ge(ve.h, M["amx1"])
            pe.inc(tensor.transpose(psS[0][0:1, 0:128], scl[:, 3:4], ident[:]))
            M["amx2"] = pe.n
        on_t(cx2)

        def cx3(vector):
            vector.wait_ge(pe.h, M["amx2"])
            vwait(vector); ve.inc(vector.memset(ccs[0:1, 0:8], 0.0))
            vwait(vector); ve.inc(vector.tensor_reduce(ccs[0:1, 0:1], psS[0][0:1, 0:128], AX, MAX))
            M["amx3"] = ve.n
        on_v(cx3)

        def cx4(scalar):
            scalar.wait_ge(ve.h, M["amx3"])
            scalar.dma_start(out=ccx_i.ap(), in_=ccs[0:1, 0:8]).then_inc(dCC.h, 16)
            dCC.n += 16
            M["ccx_up"] = dCC.n
        on_s(cx4)

        def cx5(gpsimd):
            gpsimd.wait_ge(dCC.h, M["ccx_up"])
            ge.inc(gpsimd.collective_compute(
                "AllReduce", MAX, replica_groups=[list(range(N_CORES))],
                ins=[ccx_i.ap().opt()], outs=[ccx_o.ap().opt()]))
            M["cc1"] = ge.n
        on_g(cx5)

        def scan(vector):
            vector.wait_ge(se.h, M["acol"])
            vector.wait_ge(pe.h, M[f"axp{GB-1}"])
            vwait(vector); ve.inc(vector.tensor_copy(bwt[:, 0:D], hs[:, 0:D]))
            M["dbc"] = ve.n
            for b in range(SEQ_PER_CORE):
                vwait(vector); ve.inc(vector.tensor_tensor_scan(
                    hs[:, b * L:(b + 1) * L],
                    acol[:, 0:1].broadcast_to([128, L]),
                    xproj[:, b * L:(b + 1) * L],
                    0.0, MULT, ADD))
            M["scan"] = ve.n
        on_v(scan)


        def cx6(scalar):
            scalar.wait_ge(ge.h, M["cc1"])
            scalar.dma_start(out=ccs[0:1, 8:16], in_=ccx_o.ap()).then_inc(dCC.h, 16)
            dCC.n += 16
            M["ccx_dn"] = dCC.n
        on_s(cx6)

        def cx7(tensor):
            tensor.wait_ge(dCC.h, M["ccx_dn"])
            pe.inc(tensor.matmul(psS[0][:, 0:1], ones[0:1, :], ccs[0:1, 8:9],
                                 start=True, stop=True))
            M["sxb"] = pe.n
        on_t(cx7)

        def cx8(vector):
            vector.wait_ge(pe.h, M["sxb"])
            vwait(vector); ve.inc(vector.reciprocal(scl[:, 0:1], psS[0][:, 0:1]))
            vwait(vector); ve.inc(vector.tensor_scalar(scl[:, 0:1], scl[:, 0:1],
                                        32767.0, None, MULT))
            vwait(vector); ve.inc(vector.tensor_scalar(dv[:, 0:KH], sj[:, 0:KH],
                                        scl[:, 0:1], None, MULT))
            vwait(vector); ve.inc(vector.reciprocal(dv[:, 0:KH], dv[:, 0:KH]))
            M["sxd"] = ve.n
        on_v(cx8)

        def bol(sync):
            sync.wait_ge(ve.h, M["dstash"])
            sync.dma_start(out=rowb[0:1, :], in_=bo_in).then_inc(dW.h, 16)
            dW.n += 16
            M["bol"] = dW.n
        on_y(bol)

        def emit_woq(i):
            def wol(scalar, i=i):
                if i <= 1:
                    scalar.wait_ge(ve.h, M[f"qqu{KH-1}"])
                else:
                    scalar.wait_ge(ve.h, M[f"woq{i-2}"])
                wbuf = qin[:, (i % 2) * D:(i % 2 + 1) * D]
                scalar.dma_start(out=wbuf,
                                 in_=wo_in[i * 128:(i + 1) * 128, :]
                                 ).then_inc(dW.h, 16)
                dW.n += 16
                M[f"wol{i}"] = dW.n
            on_s(wol)

            def woq(vector, i=i):
                vector.wait_ge(dW.h, M[f"wol{i}"])
                wbuf = qin[:, (i % 2) * D:(i % 2 + 1) * D]
                vwait(vector); ve.inc(vector.tensor_reduce(si[:, i:i + 1], wbuf, AX, MAX,
                                            apply_absolute_value=True))
                vwait(vector); ve.inc(vector.reciprocal(scl[:, 4:5], si[:, i:i + 1]))
                vwait(vector); ve.inc(vector.tensor_scalar(si[:, i:i + 1], scl[:, 4:5],
                                            127.0, None, MULT))
                vwait(vector); ve.inc(vector.tensor_scalar(wbuf, wbuf,
                                            si[:, i:i + 1], CM, MULT, ADD))
                vwait(vector); ve.inc(vector.tensor_scalar(woi[:, i * D:(i + 1) * D],
                                            wbuf, CM, None, SUB))
                M[f"woq{i}"] = ve.n
            on_v(woq)

        def emit_pieceB(g, k):
            def bnl(sync, g=g, k=k):
                if g == 0 and k == 0:
                    sync.wait_ge(ve.h, M[f"axn{TT-1}"])
                sync.wait_ge(dXN.h, M[f"axsp{g}"])
                prev = (g, k - 4) if k >= 4 else ((g - 1, KD - 4 + k) if g > 0 else None)
                if prev is not None:
                    sync.wait_ge(ve.h, M[f"blb{prev[0]}_{prev[1]}"])
                s = dXB[k % 4]
                sync.dma_start(out=xbuf[:, (k % 4) * 512:(k % 4 + 1) * 512],
                               in_=xnt_dram[g, :, k * 512:(k + 1) * 512]
                               ).then_inc(s.h, 16)
                s.n += 16
                M[f"bnl{g}_{k}"] = s.n
            on_y(bnl)

            def blb(vector, g=g, k=k):
                vector.wait_ge(dXB[k % 4].h, M[f"bnl{g}_{k}"])
                if k == 0:
                    if g <= 1:
                        vector.wait_ge(ve.h, M["sxd"])
                        vector.wait_ge(ve.h, M["scan"])
                    if g >= 2:
                        vector.wait_ge(pe.h, M[f"bmm{g-2}"])
                mh = movh if g % 2 == 0 else movh2
                xb = xbuf[:, (k % 4) * 512:(k % 4 + 1) * 512]
                xt = xtmp[:, (k % 2) * 512:(k % 2 + 1) * 512]
                hi = mh[:, k * 512:(k + 1) * 512]
                lo = mh[:, 4096 + k * 512:4096 + (k + 1) * 512]
                vwait(vector); ve.inc(vector.tensor_scalar(xt, xb, scl[:, 0:1],
                                            CM, MULT, ADD))
                vwait(vector); ve.inc(vector.tensor_scalar(xt, xt, CM, None, SUB))
                vwait(vector); ve.inc(vector.tensor_copy(hi, xt))
                vwait(vector); ve.inc(vector.tensor_tensor(lo, xt, hi, SUB))
                M[f"blb{g}_{k}"] = ve.n
            on_v(blb)

        def emit_bj(g, j):
            def bmm(tensor, g=g, j=j):
                if j == 0:
                    tensor.wait_ge(ve.h, M[f"blb{g}_{KD-1}"])
                if g == 0:
                    tensor.wait_ge(ve.h, M[f"qqu{j}"])
                prevj = (g, j - 4) if j >= 4 else ((g - 1, KH - 4 + j) if g > 0 else None)
                if prevj is not None:
                    tensor.wait_ge(se.h, M[f"bgl{prevj[0]}_{prevj[1]}"])
                    tensor.wait_ge(ve.h, M[f"brd{prevj[0]}_{prevj[1]}"])
                ps = psS[j % 4]
                mh = movh if g % 2 == 0 else movh2
                for k in range(KD):
                    w = wii[:, k * H + j * 128:k * H + j * 128 + 128]
                    pe.inc(tensor.matmul(ps[:], w, mh[:, k * 512:(k + 1) * 512],
                                         start=(k == 0), stop=False))
                    pe.inc(tensor.matmul(
                        ps[:], w, mh[:, 4096 + k * 512:4096 + (k + 1) * 512],
                        start=False, stop=(k == KD - 1)))
                M[f"bmmj{g}_{j}"] = pe.n
                if j == KH - 1:
                    M[f"bmm{g}"] = pe.n
            on_t(bmm)

            def bgl(scalar, g=g, j=j):
                scalar.wait_ge(pe.h, M[f"bmmj{g}_{j}"])
                if g == 0:
                    scalar.wait_ge(ve.h, M["sxd"])
                    if j < 1:
                        scalar.wait_ge(pe.h, M[f"qtr1_{KH-1}"])
                prevs = (g, j - 4) if j >= 4 else ((g - 1, KH - 4 + j) if g > 0 else None)
                if prevs is not None:
                    scalar.wait_ge(dSP[j % 4].h, M[f"bsp{prevs[0]}_{prevs[1]}"])
                sl = hslab[:, (j % 4) * 512:(j % 4 + 1) * 512]
                swait(scalar); se.inc(scalar.activation(sl, psS[j % 4][:], AF.Gelu,
                                         bias=bint[:, j:j + 1],
                                         scale=dv[:, j:j + 1]))
                M[f"bgl{g}_{j}"] = se.n
            on_s(bgl)

            def bspill(sync, g=g, j=j):
                sync.wait_ge(se.h, M[f"bgl{g}_{j}"])
                sl = hslab[:, (j % 4) * 512:(j % 4 + 1) * 512]
                s = dSP[j % 4]
                sync.dma_start(out=hid_dram[j, g, :, :], in_=sl).then_inc(s.h, 16)
                s.n += 16
                M[f"bsp{g}_{j}"] = s.n
            on_s(bspill)

            def brd(vector, g=g, j=j):
                vector.wait_ge(se.h, M[f"bgl{g}_{j}"])
                sl = hslab[:, (j % 4) * 512:(j % 4 + 1) * 512]
                vwait(vector); ve.inc(vector.tensor_reduce(amhg[:, j:j + 1], sl, AX, MAX,
                                            apply_absolute_value=True))
                M[f"brd{g}_{j}"] = ve.n
            on_v(brd)

        for g in range(GB):
            if g == 0:
                for k in range(KD):
                    emit_pieceB(0, k)
            for j in range(KH):
                emit_bj(g, j)
                if g + 1 < GB and j >= 8 and (j - 8) % 2 == 0 and (j - 8) // 2 < KD:
                    emit_pieceB(g + 1, (j - 8) // 2)
                if GB > 1 and g == 1:
                    emit_woq(j)

            def bam(vector, g=g):
                if g == 0:
                    vwait(vector); ve.inc(vector.tensor_copy(amh[:, 0:KH], amhg[:, 0:KH]))
                else:
                    vwait(vector); ve.inc(vector.tensor_tensor(amh[:, 0:KH], amh[:, 0:KH],
                                                amhg[:, 0:KH], MAX))
                M[f"bam{g}"] = ve.n
            on_v(bam)
            if GB == 1 and g == 0:
                for j in range(KH):
                    emit_woq(j)
        M["woq_all"] = M[f"woq{KH-1}"]

        def ch1(vector):
            vwait(vector); ve.inc(vector.tensor_reduce(scl[:, 5:6], amh[:, 0:KH], AX, MAX))
            M["amh1"] = ve.n
        on_v(ch1)

        def ch2(tensor):
            tensor.wait_ge(ve.h, M["amh1"])
            pe.inc(tensor.transpose(psS[0][0:1, 0:128], scl[:, 5:6], ident[:]))
            M["amh2"] = pe.n
        on_t(ch2)

        def ch3(vector):
            vector.wait_ge(pe.h, M["amh2"])
            vwait(vector); ve.inc(vector.memset(ccs[0:1, 0:8], 0.0))
            vwait(vector); ve.inc(vector.tensor_reduce(ccs[0:1, 0:1], psS[0][0:1, 0:128], AX, MAX))
            M["amh3"] = ve.n
        on_v(ch3)

        def ch4(scalar):
            scalar.wait_ge(ve.h, M["amh3"])
            scalar.dma_start(out=cch_i.ap(), in_=ccs[0:1, 0:8]).then_inc(dCC.h, 16)
            dCC.n += 16
            M["cch_up"] = dCC.n
        on_s(ch4)

        def ch5(gpsimd):
            gpsimd.wait_ge(dCC.h, M["cch_up"])
            ge.inc(gpsimd.collective_compute(
                "AllReduce", MAX, replica_groups=[list(range(N_CORES))],
                ins=[cch_i.ap().opt()], outs=[cch_o.ap().opt()]))
            M["cc2"] = ge.n
        on_g(ch5)

        def ch6(scalar):
            scalar.wait_ge(ge.h, M["cc2"])
            scalar.dma_start(out=ccs[0:1, 8:16], in_=cch_o.ap()).then_inc(dCC.h, 16)
            dCC.n += 16
            M["cch_dn"] = dCC.n
        on_s(ch6)

        def ch7(tensor):
            tensor.wait_ge(dCC.h, M["cch_dn"])
            pe.inc(tensor.matmul(psS[0][:, 0:1], ones[0:1, :], ccs[0:1, 8:9],
                                 start=True, stop=True))
            M["shb"] = pe.n
        on_t(ch7)

        def ch8(vector):
            vector.wait_ge(pe.h, M["shb"])
            vwait(vector); ve.inc(vector.reciprocal(scl[:, 1:2], psS[0][:, 0:1]))
            vwait(vector); ve.inc(vector.tensor_scalar(scl[:, 1:2], scl[:, 1:2],
                                        32767.0, None, MULT))
            vwait(vector); ve.inc(vector.tensor_scalar(civ[:, 0:KH], si[:, 0:KH],
                                        scl[:, 1:2], None, MULT))
            vwait(vector); ve.inc(vector.reciprocal(civ[:, 0:KH], civ[:, 0:KH]))
            M["shd"] = ve.n
        on_v(ch8)

        def dtile(g, sub, nh):
            if g % 2 == 0:
                return psB[sub][:, nh * 512:(nh + 1) * 512]
            return psS[sub * 2 + nh][:, 0:512]

        def dpre(tensor):
            tensor.wait_ge(ve.h, M["scan"])
            tensor.wait_ge(dW.h, M["bol"])
            for sub in range(2):
                stat = hs[:, sub * 128:sub * 128 + 128]
                for nh in range(2):
                    pe.inc(tensor.matmul(
                        psB[sub][:, nh * 512:(nh + 1) * 512],
                        stat, cwt[:, nh * 512:(nh + 1) * 512],
                        start=True, stop=False))
                for nh in range(2):
                    pe.inc(tensor.matmul(
                        psB[sub][:, nh * 512:(nh + 1) * 512],
                        ones[0:1, :], rowb[0:1, nh * 512:(nh + 1) * 512],
                        start=False, stop=False))
        on_t(dpre)

        deferred = []
        for g in range(GD):
            tok0 = g * 256
            gB, hB = tok0 // 512, (tok0 % 512) // 256

            def dxl(sync, g=g, tok0=tok0):
                if g == 0:
                    sync.wait_ge(ve.h, M[f"blb{GB-1}_{KD-1}"])
                else:
                    sync.wait_ge(ve.h, M[f"dev{g-1}"])
                sync.dma_start(
                    out=xbuf[:, 0:2 * D].rearrange("p (s d) -> p s d", s=2),
                    in_=x_in[tok0:tok0 + 256, :].rearrange("(s p) d -> p s d", s=2),
                ).then_inc(dXD.h, 16)
                dXD.n += 16
                M[f"dxl{g}"] = dXD.n

            if g == 0:
                on_y(dxl)

            pend_mm = []
            for i in range(KH):
                sl = i % 2
                if i == 2:
                    if deferred:
                        deferred[0]()
                    for fn in pend_mm:
                        on_t(fn)
                    pend_mm = []
                if i == 3 and deferred:
                    deferred[1]()
                if i == min(5, KH - 3) and deferred:
                    deferred[2]()
                if i == min(7, KH - 2) and deferred:
                    deferred[3]()
                    deferred = []
                    on_y(dxl)

                def dsr(sync, g=g, gB=gB, hB=hB, i=i):
                    if g == 0 and i < 8:
                        sync.wait_ge(ve.h, M["woq_all"])
                    sync.wait_ge(dSP[i % 4].h, M[f"bsp{gB}_{i}"])
                    prev = (g, i - 8) if i >= 8 else ((g - 1, KH - 8 + i) if g > 0 else None)
                    if prev is not None:
                        sync.wait_ge(ve.h, M[f"dp{prev[0]}_{prev[1]}"])
                    s = dSR[i % 8]
                    sync.dma_start(
                        out=qin[:, (i % 8) * 256:(i % 8) * 256 + 256],
                        in_=hid_dram[i, gB, :, hB * 256:(hB + 1) * 256],
                    ).then_inc(s.h, 16)
                    s.n += 16
                    M[f"dsr{g}_{i}"] = s.n
                on_s(dsr)

                def dp_pair(vector, g=g, i0=i - 1):
                    q = (i0 // 2) % 2
                    vector.wait_ge(dSR[i0 % 8].h, M[f"dsr{g}_{i0}"])
                    vector.wait_ge(dSR[(i0 + 1) % 8].h, M[f"dsr{g}_{i0+1}"])
                    if g == 0 and i0 == 0:
                        vector.wait_ge(ve.h, M["shd"])
                        for s4 in range(4):
                            vector.wait_ge(dSP[s4].h, dSP[s4].n)
                    prev = ((g, i0 - 3) if i0 >= 4
                            else ((g - 1, KH - 3 + i0) if g > 0 else None))
                    if prev is not None:
                        vector.wait_ge(pe.h, M[f"dmm{prev[0]}_{prev[1]}"])
                    buf2 = qin[:, (i0 % 8) * 256:(i0 % 8) * 256 + 512]
                    p = pF(q)
                    vwait(vector); ve.inc(vector.tensor_scalar(p, buf2, scl[:, 1:2],
                                                CM, MULT, ADD))
                    M[f"dp{g}_{i0}"] = ve.n
                    M[f"dp{g}_{i0+1}"] = ve.n
                    vwait(vector); ve.inc(vector.tensor_scalar(p[:, 0:256], p[:, 0:256],
                                                CM, civ[:, i0:i0 + 1], SUB, MULT))
                    vwait(vector); ve.inc(vector.tensor_scalar(p[:, 256:512], p[:, 256:512],
                                                CM, civ[:, i0 + 1:i0 + 2], SUB, MULT))
                    vwait(vector); ve.inc(vector.tensor_copy(plbp(q, 0), p))
                    vwait(vector); ve.inc(vector.tensor_tensor(plbp(q, 1), p, plbp(q, 0), SUB))
                    M[f"dpl{g}_{i0}"] = ve.n
                    M[f"dpl{g}_{i0+1}"] = ve.n
                if i % 2 == 1:
                    on_v(dp_pair)

                def mk_dmm(ii, g=g, tok0=tok0):
                  def dmm(tensor, g=g, i=ii, tok0=tok0):
                    tensor.wait_ge(ve.h, M[f"dpl{g}_{i}"])
                    if i == 0 and g >= 1:
                        if g >= 2:
                            tensor.wait_ge(ve.h, M[f"devc{g-2}"])
                        else:
                            tensor.wait_ge(se.h, M[f"bgl{GB-1}_{KH-1}"])
                        for sub in range(2):
                            stat = hs[:, tok0 + sub * 128:tok0 + sub * 128 + 128]
                            for nh in range(2):
                                pe.inc(tensor.matmul(
                                    dtile(g, sub, nh),
                                    stat, cwt[:, nh * 512:(nh + 1) * 512],
                                    start=(nh >= 0), stop=False))
                            for nh in range(2):
                                pe.inc(tensor.matmul(
                                    dtile(g, sub, nh),
                                    ones[0:1, :], rowb[0:1, nh * 512:(nh + 1) * 512],
                                    start=False, stop=False))
                    q, io = (i // 2) % 2, i % 2
                    for sub in range(2):
                        for lb in range(2):
                            stat = plbp(q, lb)[:, io * 256 + sub * 128:
                                               io * 256 + sub * 128 + 128]
                            for nh in range(2):
                                pe.inc(tensor.matmul(
                                    dtile(g, sub, nh),
                                    stat,
                                    woi[:, i * D + nh * 512:i * D + (nh + 1) * 512],
                                    start=False, stop=(i == KH - 1 and lb == 1)))
                    M[f"dmm{g}_{i}"] = pe.n
                  return dmm
                if i % 2 == 1:
                    for ii in (i - 1, i):
                        if ii < 2:
                            pend_mm.append(mk_dmm(ii))
                        else:
                            on_t(mk_dmm(ii))
            M[f"dy{g}"] = M[f"dmm{g}_{KH-1}"]

            def mk_tail(g=g, tok0=tok0):
                def emit_devc():
                    def devc(vector, g=g):
                        vector.wait_ge(pe.h, M[f"dy{g}"])
                        if g == 0:
                            vector.wait_ge(pe.h, M[f"bmm{GB-1}"])
                        if g >= 1:
                            vector.wait_ge(dOUT.h, M[f"dout{g-1}"])
                        for sub in range(2):
                            ot = outst[:, sub * D:(sub + 1) * D]
                            if g % 2 == 0:
                                vwait(vector); ve.inc(vector.tensor_copy(ot, psB[sub][:]))
                            else:
                                for nh in range(2):
                                    vwait(vector); ve.inc(vector.tensor_copy(
                                        ot[:, nh * 512:(nh + 1) * 512],
                                        dtile(g, sub, nh)))
                        M[f"devc{g}"] = ve.n
                    on_v(devc)

                def mk_dev(sub):
                    def emit_dev():
                        def dev(vector, g=g, sub=sub):
                            if sub == 0:
                                vector.wait_ge(dXD.h, M[f"dxl{g}"])
                            t = g * 2 + sub
                            xt = xbuf[:, sub * D:(sub + 1) * D]
                            ot = outst[:, sub * D:(sub + 1) * D]
                            vwait(vector); ve.inc(vector.tensor_scalar(
                                xtmp[:, 0:D], xt, mu[:, t:t + 1],
                                rsig[:, t:t + 1], SUB, MULT))
                            vwait(vector); ve.inc(vector.tensor_tensor(
                                xtmp[:, 0:D], xtmp[:, 0:D], bwt[:, 0:D], MULT))
                            vwait(vector); ve.inc(vector.tensor_tensor(
                                xtmp[:, 0:D], xtmp[:, 0:D], xt, ADD))
                            vwait(vector); ve.inc(vector.tensor_tensor(
                                ot, ot, xtmp[:, 0:D], ADD))
                            if sub == 1:
                                M[f"dev{g}"] = ve.n
                        on_v(dev)
                    return emit_dev

                def emit_dout():
                    def dout(sync, g=g, tok0=tok0):
                        sync.wait_ge(ve.h, M[f"dev{g}"])
                        sync.dma_start(
                            out=y_out[tok0:tok0 + 256, :].rearrange("(s p) d -> p s d", s=2),
                            in_=outst[:, 0:2 * D].rearrange("p (s d) -> p s d", s=2),
                        ).then_inc(dOUT.h, 16)
                        dOUT.n += 16
                        M[f"dout{g}"] = dOUT.n
                    on_y(dout)
                return [emit_devc, mk_dev(0), mk_dev(1), emit_dout]
            deferred = mk_tail()

        for fn in deferred:
            fn()
        deferred = []

        if debug:
            def dbg(sync):
                sync.wait_ge(ve.h, ve.n)
                sync.wait_ge(se.h, se.n)
                sync.wait_ge(pe.h, pe.n)
                for src_ap, c0 in [(mu, 0), (rsig, TT), (rss, 2 * TT), (amx, 3 * TT),
                                   (sj, 4 * TT), (si, 4 * TT + KH), (dv, 4 * TT + 2 * KH),
                                   (civ, 4 * TT + 3 * KH)]:
                    w = src_ap.ap().shape[1]
                    sync.dma_start(out=dbg_small[:, c0:c0 + w], in_=src_ap[:, 0:w]
                                   ).then_inc(dOUT.h, 16)
                    dOUT.n += 16
                sync.dma_start(out=dbg_small[:, 248:252].rearrange("p (a b) -> p a b", a=4),
                               in_=scl[:, 0:4].rearrange("p (a b) -> p a b", a=4),
                               allow_slow_non_contiguous=True).then_inc(dOUT.h, 16)
                sync.dma_start(out=dbg_small[:, 252:253], in_=acol[:, 0:1],
                               allow_slow_non_contiguous=True).then_inc(dOUT.h, 16)
                dOUT.n += 32
                sync.dma_start(out=dbg_xproj, in_=xproj[:, 0:T]).then_inc(dOUT.h, 16)
                sync.dma_start(out=dbg_hs, in_=hs[:, 0:T]).then_inc(dOUT.h, 16)
                sync.dma_start(out=dbg_wii, in_=wii[:]).then_inc(dOUT.h, 16)
                sync.dma_start(out=dbg_woi, in_=woi[:]).then_inc(dOUT.h, 16)
                sync.dma_start(out=dbg_movb, in_=movb[:]).then_inc(dOUT.h, 16)
                dOUT.n += 80
            on_y(dbg)

        def fin(sync):
            sync.wait_ge(dOUT.h, dOUT.n)
        on_y(fin)

    return nc


def kernel(L0_attn_x, L0_attn_B_w, L0_attn_C_w, L0_attn_log_A, L0_attn_D_param,
           L0_ffn_W_in, L0_ffn_b_in, L0_ffn_W_out, L0_ffn_b_out):
    cfg = Cfg()
    B, L, D = L0_attn_x.shape
    assert (B, L, D) == (16, 2048, 1024)
    x = np.ascontiguousarray(np.asarray(L0_attn_x, dtype=np.float32))
    shards = [np.ascontiguousarray(
        x[c * SEQ_PER_CORE:(c + 1) * SEQ_PER_CORE].reshape(cfg.T, D))
        for c in range(N_CORES)]
    common = {
        "bw": np.ascontiguousarray(np.asarray(L0_attn_B_w, np.float32)),
        "cw": np.ascontiguousarray(np.asarray(L0_attn_C_w, np.float32)),
        "la": np.ascontiguousarray(np.asarray(L0_attn_log_A, np.float32).reshape(1, 128)),
        "dp": np.ascontiguousarray(np.asarray(L0_attn_D_param, np.float32).reshape(1, D)),
        "wi": np.ascontiguousarray(np.asarray(L0_ffn_W_in, np.float32)),
        "bi": np.ascontiguousarray(np.asarray(L0_ffn_b_in, np.float32).reshape(cfg.KH, 128)),
        "wo": np.ascontiguousarray(np.asarray(L0_ffn_W_out, np.float32)),
        "bo": np.ascontiguousarray(np.asarray(L0_ffn_b_out, np.float32).reshape(1, D)),
        "ident": np.eye(128, dtype=np.float32),
    }
    nc = build(cfg)
    in_maps = [dict(common, x=s) for s in shards]
    res = run_bass_kernel_spmd(nc, in_maps, core_ids=list(range(N_CORES)))
    out = np.concatenate([np.asarray(r["y"]).reshape(SEQ_PER_CORE, L, D)
                          for r in res.results], axis=0)
    return out
